# version 04
# baseline (speedup 1.0000x reference)
import numpy as np
import ml_dtypes
from contextlib import ExitStack

import concourse.bass as bass
import concourse.mybir as mybir
from concourse.bass_utils import run_bass_kernel_spmd

F32 = mybir.dt.float32
BF16 = mybir.dt.bfloat16
AF = mybir.ActivationFunctionType
ALU = mybir.AluOpType
AX = mybir.AxisListType
BFNP = ml_dtypes.bfloat16

D = 1024
T = 4096
NCORE = 8
NE = 8
FF = 4096
NH = 16
DH = 64
NEG = -1e30

ENGS = ("sync", "gpsimd", "tensor", "scalar", "vector")
SEMS = ("q", "g", "t", "s", "v", "gc")
ENG_SEM = {"sync": "q", "gpsimd": "g", "tensor": "t", "scalar": "s", "vector": "v"}


class C:
    pass


class Ph:

    def __init__(self, c):
        self.c = c
        self.items = {e: [] for e in ENGS}
        self.ev = {}

    def emit(self, eng, fn, waits=(), ev=None, inc=None, sem=None):
        self.items[eng].append((fn, tuple(waits), ev, inc, sem))

    def dma(self, eng, fn, waits=(), ev=None):
        self.emit(eng, fn, waits, ev, inc=16)

    def seal(self):
        for eng in ENGS:
            for (fn, waits, ev, inc, semo) in self.items[eng]:
                if fn is None:
                    continue
                sem = semo or ENG_SEM[eng]
                amt = inc if inc is not None else 1
                if amt == 0:
                    assert ev is None
                    continue
                self.c.semn[sem] += amt
                if ev is not None:
                    assert ev not in self.ev, f"duplicate event {ev}"
                    self.ev[ev] = (sem, self.c.semn[sem])

    def check(self):
        ptr = {e: 0 for e in ENGS}
        cnt = dict(self.base)
        progress = True
        while progress:
            progress = False
            for eng in ENGS:
                while ptr[eng] < len(self.items[eng]):
                    fn, waits, ev, inc, semo = self.items[eng][ptr[eng]]
                    ok = all(cnt[self.ev[w][0]] >= self.ev[w][1] for w in waits)
                    if not ok:
                        break
                    if fn is not None:
                        amt = inc if inc is not None else 1
                        cnt[semo or ENG_SEM[eng]] += amt
                    ptr[eng] += 1
                    progress = True
        stuck = {e: ptr[e] for e in ENGS if ptr[e] < len(self.items[e])}
        if stuck:
            for e, p in stuck.items():
                fn, waits, ev, inc = self.items[e][p]
                bad = [w for w in waits
                       if cnt[self.ev[w][0]] < self.ev[w][1]]
                raise RuntimeError(f"deadlock: engine {e} item {p} ev={ev} "
                                   f"blocked on {bad}")

    def build(self):
        for w_list in (w for e in ENGS for (_, w, _, _, _) in self.items[e]):
            for w in w_list:
                assert w in self.ev, f"unknown event {w}"
        self.check()
        c = self.c
        with c.nc.Block() as block:
            for eng in ENGS:
                if not self.items[eng]:
                    continue

                def body(engine, eng=eng):
                    maxw = {}
                    for (fn, waits, ev, inc, semo) in self.items[eng]:
                        tgt = {}
                        for w in waits:
                            s, vv = self.ev[w]
                            tgt[s] = max(tgt.get(s, 0), vv)
                        for s, vv in tgt.items():
                            if vv > maxw.get(s, 0):
                                engine.wait_ge(c.semh[s], vv)
                                maxw[s] = vv
                        if fn is None:
                            continue
                        ins = fn(engine)
                        amt = inc if inc is not None else 1
                        if amt:
                            ins.then_inc(c.semh[semo or ENG_SEM[eng]], amt)
                getattr(block, eng)(body)


def new_phase(c):
    ph = Ph(c)
    ph.base = dict(c.semn)
    c.uid = getattr(c, "uid", 0) + 1
    return ph



def _causal_mask_l0(off):
    i = np.arange(128)[:, None]
    j = np.arange(256)[None, :]
    return np.where(off + i <= j, 0.0, NEG).astype(np.float32)


def _win_mask_l1(off):
    i = np.arange(128)[:, None]
    j = np.arange(256)[None, :]
    d = off + i - j
    return np.where((d <= 0) & (d >= -63), 0.0, NEG).astype(np.float32)


def _init_mask_l1():
    i = np.arange(64)[:, None]
    j = np.arange(256)[None, :]
    m = np.where(j <= i, 0.0, NEG).astype(np.float32)
    m[63, :] = NEG
    return m


def prep_common(inputs):
    f = lambda a: np.ascontiguousarray(np.asarray(a), dtype=np.float32)
    c = {}
    c["l0_wq"] = f(np.asarray(inputs["L0_attn_Wq"]).T)
    c["l0_wk"] = f(np.asarray(inputs["L0_attn_Wk"]).T)
    c["l0_wv"] = f(np.asarray(inputs["L0_attn_Wv"]).T)
    c["l0_woT"] = f(np.asarray(inputs["L0_attn_Wo"]).T)
    c["l0_wg"] = f(inputs["L0_ffn_Wg"])
    c["l0_wu"] = f(inputs["L0_ffn_Wu"])
    c["l0_wd"] = f(inputs["L0_ffn_Wd"])
    c["l1_wqT"] = f(np.asarray(inputs["L1_attn_Wq"]).T)
    c["l1_wkT"] = f(np.asarray(inputs["L1_attn_Wk"]).T)
    c["l1_wvT"] = f(np.asarray(inputs["L1_attn_Wv"]).T)
    c["l1_woT"] = f(np.asarray(inputs["L1_attn_Wo"]).T)
    ki = np.asarray(inputs["L1_attn_last_k_init"])
    vi = np.asarray(inputs["L1_attn_last_v_init"])
    c["l1_ki"] = f(np.transpose(ki, (1, 2, 0)))
    c["l1_vi"] = f(np.transpose(vi, (1, 0, 2)))
    c["l1_w1T"] = f(np.asarray(inputs["L1_ffn_w1"]).T)
    c["l1_w2T"] = f(np.asarray(inputs["L1_ffn_w2"]).T)
    c["l1_b1"] = f(np.asarray(inputs["L1_ffn_b1"]).reshape(32, 128, 1))
    c["l1_b2"] = f(np.asarray(inputs["L1_ffn_b2"]).reshape(8, 128, 1))
    c["ones"] = np.ones((128, 128), dtype=BFNP)
    c["ident"] = np.eye(128, dtype=BFNP)
    c["l0_mask"] = np.stack([_causal_mask_l0(0), _causal_mask_l0(128)])
    c["l1_mask"] = np.stack([_win_mask_l1(-128), _win_mask_l1(0), _win_mask_l1(128)])
    c["l1_maski"] = _init_mask_l1()
    return c


COMMON_SPECS = [
    ("l0_wq", [64, 64], F32), ("l0_wk", [64, 64], F32), ("l0_wv", [64, 64], F32),
    ("l0_woT", [D, D], F32),
    ("l0_wg", [NE, D, FF], F32), ("l0_wu", [NE, D, FF], F32), ("l0_wd", [NE, FF, D], F32),
    ("l1_wqT", [D, D], F32), ("l1_wkT", [D, D], F32), ("l1_wvT", [D, D], F32), ("l1_woT", [D, D], F32),
    ("l1_ki", [NH, 64, 63], F32), ("l1_vi", [NH, 63, 64], F32),
    ("l1_w1T", [D, FF], F32), ("l1_w2T", [FF, D], F32),
    ("l1_b1", [32, 128, 1], F32), ("l1_b2", [8, 128, 1], F32),
    ("ones", [128, 128], BF16), ("ident", [128, 128], BF16),
    ("l0_mask", [2, 128, 256], F32), ("l1_mask", [3, 128, 256], F32),
    ("l1_maski", [64, 256], F32),
]

PHASE_INPUTS = {
    "ln": [],
    "pre_l0ffn": ["l0_wg", "l0_wu", "l0_wd"],
    "pre_l0attn": ["l0_wq", "l0_wk", "l0_wv", "l0_woT"],
    "pre_l1attn": ["l1_wqT", "l1_wkT", "l1_wvT", "l1_woT", "l1_ki", "l1_vi"],
    "pre_l1ffn": ["l1_w1T", "l1_w2T", "l1_b1", "l1_b2"],
    "l0ffn": [], "l0attn": ["l0_mask"], "l1attn": ["l1_mask", "l1_maski"], "l1ffn": [],
}
ALWAYS_INPUTS = ["ones", "ident"]



def ph_init(c, needed):
    ph = new_phase(c)
    loads = [(c.xt.rearrange("p (dc t) -> p dc t", t=T),
              c.xT.rearrange("(dc p) t -> p dc t", p=128))]
    loads.append((c.ones[:], c.dram["ones"][:]))
    loads.append((c.ident[:], c.dram["ident"][:]))
    if "l0_mask" in c.dram:
        loads.append((c.l0m.rearrange("p (vv j) -> p vv j", j=256),
                      c.dram["l0_mask"].rearrange("vv p j -> p vv j")))
    if "l1_mask" in c.dram:
        loads.append((c.l1m.rearrange("p (vv j) -> p vv j", j=256),
                      c.dram["l1_mask"].rearrange("vv p j -> p vv j")))
        loads.append((c.l1mi[:], c.dram["l1_maski"][:]))
    for i, (dst, src) in enumerate(loads):
        ph.dma("sync", lambda e, dst=dst, src=src: e.dma_start(out=dst, in_=src),
               ev=f"ini_{i}")
    ph.emit("sync", None, waits=[f"ini_{len(loads)-1}"])
    ph.emit("vector", lambda e: e.memset(c.eps[:], 1e-5), ev="ini_eps")
    ph.emit("vector", lambda e: e.memset(c.tiny[:], 1e-30), ev="ini_tiny")
    ph.seal()
    ph.build()


def ph_store(c):
    ph = new_phase(c)
    ph.dma("sync", lambda e: e.dma_start(
        out=c.outT.rearrange("(dc p) t -> p dc t", p=128),
        in_=c.xt.rearrange("p (dc t) -> p dc t", t=T)), ev="sto")
    ph.emit("sync", None, waits=["sto"])
    ph.seal()
    ph.build()


def ph_pre_l0ffn(c):
    nc = c.nc
    items = ([("wg", e, i) for e in range(NE) for i in range(8)]
             + [("wu", e, i) for e in range(NE) for i in range(8)]
             + [("wd", e, i) for e in range(NE) for i in range(8)])
    n = len(items)

    def src_ap(it):
        k, e, i = it
        if k == "wg":
            return c.dram["l0_wg"][e, i * 128:(i + 1) * 128, :]
        if k == "wu":
            return c.dram["l0_wu"][e, i * 128:(i + 1) * 128, :]
        return c.dram["l0_wd"][e, i * 512:(i + 1) * 512, :].rearrange("(s p) w -> p s w", p=128)

    def dst_ap(it):
        k, e, i = it
        if k == "wg":
            return c.wg_bf[e, :, :, i * 128:(i + 1) * 128].rearrange("f p j -> p f j")
        if k == "wu":
            return c.wu_bf[e, :, :, i * 128:(i + 1) * 128].rearrange("f p j -> p f j")
        return c.wd_bf[e, i * 512:(i + 1) * 512, :].rearrange("(s p) w -> p s w", p=128)

    with ExitStack() as ps:
        fbuf = [ps.enter_context(nc.sbuf_tensor(f"ppf{i}_{c.uid}", [128, 4096], F32)) for i in range(2)]
        cbuf = [ps.enter_context(nc.sbuf_tensor(f"ppc{i}_{c.uid}", [128, 4096], BF16)) for i in range(2)]
        ph = new_phase(c)
        for i, it in enumerate(items):
            w = [f"ppv_{i-2}"] if i >= 2 else []
            ph.dma("sync", lambda e, i=i, it=it: e.dma_start(out=fbuf[i % 2][:], in_=src_ap(it)),
                   waits=w, ev=f"ppl_{i}")
            w = [f"ppl_{i}"] + ([f"pps_{i-2}"] if i >= 2 else [])
            if it[0] == "wd":
                ph.emit("vector", lambda e, i=i: e.tensor_scalar_mul(cbuf[i % 2][:], fbuf[i % 2][:], 0.5),
                        waits=w, ev=f"ppv_{i}")
            else:
                ph.emit("vector", lambda e, i=i: e.tensor_copy(cbuf[i % 2][:], fbuf[i % 2][:]),
                        waits=w, ev=f"ppv_{i}")
            inner = 128 if it[0] != "wd" else 1024
            ph.dma("gpsimd", lambda e, i=i, it=it, inner=inner: e.dma_start(
                out=dst_ap(it), in_=cbuf[i % 2].rearrange("p (a b) -> p a b", b=inner)),
                waits=[f"ppv_{i}"], ev=f"pps_{i}")
        ph.emit("gpsimd", None, waits=[f"pps_{n-1}"])
        ph.seal()
        ph.build()


def ph_pre_small(c, which):
    nc = c.nc
    items = []
    if which == "l0attn":
        for dc in range(8):
            items.append((c.dram["l0_woT"][dc * 128:(dc + 1) * 128, :],
                          c.l0_woT_bf[dc * 128:(dc + 1) * 128, :], 128, 1024))
        items.append((c.dram["l0_wq"][:], c.l0wq[:], 64, 64))
        items.append((c.dram["l0_wk"][:], c.l0wk[:], 64, 64))
        items.append((c.dram["l0_wv"][:], c.l0wv[:], 64, 64))
    else:
        for nm_src, nm_dst in [("l1_wqT", "l1_wqT_bf"), ("l1_wkT", "l1_wkT_bf"),
                               ("l1_wvT", "l1_wvT_bf"), ("l1_woT", "l1_woT_bf")]:
            for dc in range(8):
                items.append((c.dram[nm_src][dc * 128:(dc + 1) * 128, :],
                              getattr(c, nm_dst)[dc * 128:(dc + 1) * 128, :], 128, 1024))
        for h in range(NH):
            items.append((c.dram["l1_ki"][h], c.ki_bf[h], 64, 63))
            items.append((c.dram["l1_vi"][h], c.vi_bf[h], 63, 64))
    n = len(items)
    with ExitStack() as ps:
        fbuf = [ps.enter_context(nc.sbuf_tensor(f"psf{i}_{c.uid}", [128, 1024], F32)) for i in range(2)]
        cbuf = [ps.enter_context(nc.sbuf_tensor(f"psc{i}_{c.uid}", [128, 1024], BF16)) for i in range(2)]
        ph = new_phase(c)
        for i, (src, dst, pp, ff) in enumerate(items):
            w = [f"qsv_{i-2}"] if i >= 2 else []
            ph.dma("sync", lambda e, i=i, src=src, pp=pp, ff=ff: e.dma_start(
                out=fbuf[i % 2][0:pp, 0:ff], in_=src), waits=w, ev=f"qsl_{i}")
            w = [f"qsl_{i}"] + ([f"qss_{i-2}"] if i >= 2 else [])
            ph.emit("vector", lambda e, i=i, pp=pp, ff=ff: e.tensor_copy(
                cbuf[i % 2][0:pp, 0:ff], fbuf[i % 2][0:pp, 0:ff]), waits=w, ev=f"qsv_{i}")
            ph.dma("gpsimd", lambda e, i=i, dst=dst, pp=pp, ff=ff: e.dma_start(
                out=dst, in_=cbuf[i % 2][0:pp, 0:ff]), waits=[f"qsv_{i}"], ev=f"qss_{i}")
        ph.emit("gpsimd", None, waits=[f"qss_{n-1}"])
        ph.seal()
        ph.build()


def ph_pre_l1ffn(c):
    nc = c.nc
    items = [("w1", i) for i in range(8)] + [("w2", i) for i in range(32)]

    def w_src(it):
        k, i = it
        return (c.dram["l1_w1T"] if k == "w1" else c.dram["l1_w2T"])[i * 128:(i + 1) * 128, :]

    def w_dst(it):
        k, i = it
        if k == "w1":
            return c.w1s[:, :, i * 128:(i + 1) * 128].rearrange("h p j -> p h j")
        return c.w2s[:, :, i * 128:(i + 1) * 128].rearrange("w p j -> p w j")

    def w_ff(it):
        return 4096 if it[0] == "w1" else 1024

    n = len(items)
    with ExitStack() as ps:
        fbuf = [ps.enter_context(nc.sbuf_tensor(f"pqf{i}_{c.uid}", [128, 4096], F32)) for i in range(2)]
        abuf = [ps.enter_context(nc.sbuf_tensor(f"pqa{i}_{c.uid}", [128, 4096], BF16)) for i in range(2)]
        gbuf = [ps.enter_context(nc.sbuf_tensor(f"pqg0_{c.uid}", [128, 4096], BF16))] * 2
        arow = ps.enter_context(nc.sbuf_tensor(f"pqrow_{c.uid}", [1, 4096], F32))
        pall = ps.enter_context(nc.psum_tensor(f"pq_ps_{c.uid}", [128, 4096], F32))
        ph = new_phase(c)

        def emit_item(i, it):
            ff = w_ff(it)
            k, ic = it
            w = [f"qfa_{i-2}", f"qfg_{i-2}"] if i >= 2 else []
            ph.dma("sync", lambda e, i=i, it=it, ff=ff: e.dma_start(
                out=fbuf[i % 2][:, 0:ff], in_=w_src(it)), waits=w, ev=f"qfl_{i}")
            w = [f"qfl_{i}"] + ([f"qfm_{i-2}"] if i >= 2 else [])
            ph.emit("scalar", lambda e, i=i, ff=ff: e.activation(
                abuf[i % 2][:, 0:ff], fbuf[i % 2][:, 0:ff], AF.Abs),
                waits=w, ev=f"qfa_{i}")
            w = [f"qfl_{i}"] + ([f"qfs_{i-1}"] if i >= 1 else [])
            ph.emit("scalar", lambda e, i=i, ff=ff: e.activation(
                gbuf[i % 2][:, 0:ff], fbuf[i % 2][:, 0:ff], AF.Sign),
                waits=w, ev=f"qfg_{i}")
            ncol = 8 if k == "w1" else 2
            for cc in range(ncol):
                last = (cc == ncol - 1)
                w2w = [f"qfa_{i}"]
                if k == "w2" and ic == 0 and cc == 0:
                    w2w.append("qfe1")
                ph.emit("tensor", lambda e, i=i, cc=cc, k=k, ic=ic: e.matmul(
                    pall[:, cc * 512:(cc + 1) * 512] if k == "w1"
                    else pall[:, 2048 + cc * 512: 2048 + (cc + 1) * 512],
                    c.ones[:], abuf[i % 2][:, cc * 512:(cc + 1) * 512],
                    start=(ic == 0), stop=(ic == (7 if k == "w1" else 31))),
                    waits=w2w, ev=(f"qfm_{i}" if last else None))
            ph.dma("gpsimd", lambda e, i=i, it=it, ff=ff: e.dma_start(
                out=w_dst(it), in_=gbuf[i % 2][:, 0:ff].rearrange("p (a j) -> p a j", j=128)),
                waits=[f"qfg_{i}"], ev=f"qfs_{i}")

        for i in range(8):
            emit_item(i, items[i])
        for cc in range(8):
            ph.emit("vector", lambda e, cc=cc: e.tensor_scalar_mul(
                arow[:, cc * 512:(cc + 1) * 512], pall[0:1, cc * 512:(cc + 1) * 512], 1.0 / 1024),
                waits=[f"qfm_{7}"], ev=(f"qfe1" if cc == 7 else None))
        ph.dma("gpsimd", lambda e: e.dma_start(
            out=c.alpha1d.rearrange("a p x -> x (a p)"), in_=arow[:]),
            waits=["qfe1"], ev="qfe1d")
        for i in range(8, n):
            emit_item(i, items[i])
        for cc in range(2):
            ph.emit("vector", lambda e, cc=cc: e.tensor_scalar_mul(
                arow[:, cc * 512:(cc + 1) * 512], pall[0:1, 2048 + cc * 512: 2048 + (cc + 1) * 512],
                0.5 / 4096), waits=[f"qfm_{n-1}", "qfe1d"], ev=(f"qfe2" if cc == 1 else None))
        ph.dma("gpsimd", lambda e: e.dma_start(
            out=c.alpha2d.rearrange("a p x -> x (a p)"), in_=arow[0:1, 0:1024]),
            waits=["qfe2"], ev="qfe2d")
        k = 0
        for hc in range(32):
            ph.dma("sync", lambda e, hc=hc: e.dma_start(out=c.a1[:, hc:hc + 1], in_=c.alpha1d[hc]),
                   waits=["qfe1d"], ev=f"qfr_{k}"); k += 1
            ph.dma("sync", lambda e, hc=hc: e.dma_start(out=c.b1[:, hc:hc + 1], in_=c.dram["l1_b1"][hc]),
                   ev=f"qfr_{k}"); k += 1
        for cc in range(8):
            ph.dma("sync", lambda e, cc=cc: e.dma_start(out=c.a2h[:, cc:cc + 1], in_=c.alpha2d[cc]),
                   waits=["qfe2d"], ev=f"qfr_{k}"); k += 1
            ph.dma("sync", lambda e, cc=cc: e.dma_start(out=c.b2h[:, cc:cc + 1], in_=c.dram["l1_b2"][cc]),
                   ev=f"qfr_{k}"); k += 1
        ph.emit("vector", lambda e: e.tensor_scalar_mul(c.b2h[:], c.b2h[:], 0.5),
                waits=[f"qfr_{k-1}"], ev="qfb2")
        ph.emit("gpsimd", None, waits=["qfb2", f"qfs_{n-1}"])
        ph.seal()
        ph.build()


def ph_ln(c, tag):
    nc = c.nc
    with ExitStack() as ps:
        xbf = [ps.enter_context(nc.sbuf_tensor(f"lnx{i}_{c.uid}", [128, 512], BF16)) for i in range(2)]
        sq = [ps.enter_context(nc.sbuf_tensor(f"lnq{i}_{c.uid}", [128, 512], BF16)) for i in range(2)]
        msb = [ps.enter_context(nc.sbuf_tensor(f"lnm{i}_{c.uid}", [128, 512], F32)) for i in range(2)]
        msq = ps.enter_context(nc.sbuf_tensor(f"lnmsq_{c.uid}", [128, 512], F32))
        var = [ps.enter_context(nc.sbuf_tensor(f"lnv{i}_{c.uid}", [128, 512], F32)) for i in range(2)]
        rst = [ps.enter_context(nc.sbuf_tensor(f"lnr{i}_{c.uid}", [128, 512], F32)) for i in range(2)]
        t1 = [ps.enter_context(nc.sbuf_tensor(f"lnt{i}_{c.uid}", [128, 512], F32)) for i in range(2)]
        yb = [ps.enter_context(nc.sbuf_tensor(f"lny{i}_{c.uid}", [128, 4096], BF16)) for i in range(2)]
        ps_m = [ps.enter_context(nc.psum_tensor(f"lnpm{i}_{c.uid}", [128, 512], F32)) for i in range(2)]
        ps_s = [ps.enter_context(nc.psum_tensor(f"lnps{i}_{c.uid}", [128, 512], F32)) for i in range(2)]

        def xsl(dc, tb):
            return c.xt[:, dc * T + tb * 512: dc * T + tb * 512 + 512]

        ph = new_phase(c)
        g = tag
        for tb in range(8):
            for dc in range(8):
                i = tb * 8 + dc
                w = [f"{g}mm_{i-2}", f"{g}sq_{i-2}"] if i >= 2 else []
                ph.emit("scalar", lambda e, dc=dc, tb=tb, i=i: e.copy(xbf[i % 2][:], xsl(dc, tb)),
                        waits=w, ev=f"{g}cp_{i}")
                w = [f"{g}cp_{i}"] + ([f"{g}sm_{i-2}"] if i >= 2 else [])
                ph.emit("vector", lambda e, i=i: e.tensor_mul(sq[i % 2][:], xbf[i % 2][:], xbf[i % 2][:]),
                        waits=w, ev=f"{g}sq_{i}")
                w = [f"{g}cp_{i}"] + ([f"{g}st1_{tb-2}"] if tb >= 2 else [])
                ph.emit("tensor", lambda e, i=i, dc=dc, tb=tb: e.matmul(
                    ps_m[tb % 2][:], c.ones[:], xbf[i % 2][:], start=(dc == 0), stop=(dc == 7)),
                    waits=w, ev=f"{g}mm_{i}")
                w = [f"{g}sq_{i}"] + ([f"{g}st2_{tb-2}"] if tb >= 2 else [])
                ph.emit("tensor", lambda e, i=i, dc=dc, tb=tb: e.matmul(
                    ps_s[tb % 2][:], c.ones[:], sq[i % 2][:], start=(dc == 0), stop=(dc == 7)),
                    waits=w, ev=f"{g}sm_{i}")
            ph.emit("vector", lambda e, tb=tb: e.tensor_scalar_mul(msb[tb % 2][:], ps_m[tb % 2][:], 1.0 / 1024),
                    waits=[f"{g}mm_{8*tb+7}"], ev=f"{g}st1_{tb}")
            ph.emit("vector", lambda e, tb=tb: e.tensor_scalar_mul(var[tb % 2][:], ps_s[tb % 2][:], 1.0 / 1024),
                    waits=[f"{g}sm_{8*tb+7}"], ev=f"{g}st2_{tb}")
            ph.emit("vector", lambda e, tb=tb: e.tensor_mul(msq[:], msb[tb % 2][:], msb[tb % 2][:]),
                    waits=[f"{g}st1_{tb}", f"{g}st2_{tb}"], ev=f"{g}m2_{tb}")
            ph.emit("vector", lambda e, tb=tb: e.tensor_sub(var[tb % 2][:], var[tb % 2][:], msq[:]),
                    waits=[f"{g}m2_{tb}"], ev=f"{g}zz_{tb}")
            ph.emit("scalar", lambda e, tb=tb: e.activation(var[tb % 2][:], var[tb % 2][:], AF.Ln,
                                                            bias=c.eps[:]),
                    waits=[f"{g}zz_{tb}"], ev=f"{g}ln_{tb}")
            ph.emit("scalar", lambda e, tb=tb: e.activation(rst[tb % 2][:], var[tb % 2][:], AF.Exp,
                                                            scale=-0.5),
                    waits=[f"{g}ln_{tb}"], ev=f"{g}rs_{tb}")
            for dc in range(8):
                w = [f"{g}st1_{tb}"] + ([f"{g}yw_{tb-2}"] if tb >= 2 else [])
                if dc >= 2:
                    w.append(f"{g}ya_{tb}_{dc-2}")
                elif tb > 0:
                    w.append(f"{g}ya_{tb-1}_{dc+6}")
                ph.emit("vector", lambda e, dc=dc, tb=tb: e.tensor_sub(t1[dc % 2][:], xsl(dc, tb), msb[tb % 2][:]),
                        waits=w, ev=f"{g}t1_{tb}_{dc}")
                ph.emit("vector", lambda e, dc=dc, tb=tb: e.tensor_mul(
                    yb[tb % 2][:, dc * 512:(dc + 1) * 512], t1[dc % 2][:], rst[tb % 2][:]),
                    waits=[f"{g}t1_{tb}_{dc}", f"{g}rs_{tb}"], ev=f"{g}ya_{tb}_{dc}")
            ph.dma("gpsimd", lambda e, tb=tb: e.dma_start(
                out=c.yT[:, tb * 512:(tb + 1) * 512].rearrange("(dc p) t -> p dc t", p=128),
                in_=yb[tb % 2].rearrange("p (dc t) -> p dc t", t=512)),
                waits=[f"{g}ya_{tb}_{7}"], ev=f"{g}yw_{tb}")
        ph.emit("gpsimd", None, waits=[f"{g}yw_{7}"])
        ph.seal()
        ph.build()


def ph_l0ffn(c, tag):
    nc = c.nc
    g = tag
    with ExitStack() as ps:
        wgb = [ps.enter_context(nc.sbuf_tensor(f"wgb{i}_{c.uid}", [128, 1024], BF16)) for i in range(4)]
        wub = [ps.enter_context(nc.sbuf_tensor(f"wub{i}_{c.uid}", [128, 1024], BF16)) for i in range(4)]
        wdb = [ps.enter_context(nc.sbuf_tensor(f"wdb{i}_{c.uid}", [128, 1024], BF16)) for i in range(4)]
        ybuf = ps.enter_context(nc.sbuf_tensor(f"ybuf_{c.uid}", [128, 4096], BF16))
        slg = [ps.enter_context(nc.sbuf_tensor(f"slg{i}_{c.uid}", [128, 512], BF16)) for i in range(2)]
        hbuf = ps.enter_context(nc.sbuf_tensor(f"hbuf_{c.uid}", [128, 32 * 512], BF16))
        pall = ps.enter_context(nc.psum_tensor(f"ff_ps_{c.uid}", [128, 4096], F32))

        def bank(b):
            return pall[:, b * 512:(b + 1) * 512]

        ph = new_phase(c)
        for it in range(64):
            tb, e = it // 8, it % 8
            if e == 0:
                w = [f"{g}u_{32*it-1}"] if it > 0 else []
                ph.dma("gpsimd", lambda e_, tb=tb: e_.dma_start(
                    out=ybuf.rearrange("p (dc t) -> p dc t", t=512),
                    in_=c.yT[:, tb * 512:(tb + 1) * 512].rearrange("(dc p) t -> p dc t", p=128)),
                    waits=w, ev=f"{g}y_{tb}")
            for f in range(32):
                idx = it * 32 + f
                w = [f"{g}g_{idx-4}"] if idx >= 4 else []
                ph.dma("sync", lambda s, e=e, f=f: s.dma_start(out=wgb[f % 4][:], in_=c.wg_bf[e, f]),
                       waits=w, ev=f"{g}lg_{idx}")
                w = [f"{g}u_{idx-4}"] if idx >= 4 else []
                ph.dma("sync", lambda s, e=e, f=f: s.dma_start(out=wub[f % 4][:], in_=c.wu_bf[e, f]),
                       waits=w, ev=f"{g}lu_{idx}")
                if idx >= 4:
                    pidx = idx - 4
                    if pidx % 32 == 31:
                        w = [f"{g}bD_{pidx // 32}_7"]
                    else:
                        w = [f"{g}b_{pidx}"]
                else:
                    w = []
                ph.dma("gpsimd", lambda s, e=e, f=f: s.dma_start(
                    out=wdb[f % 4][:], in_=c.wd_bf[e, f * 128:(f + 1) * 128, :]),
                    waits=w, ev=f"{g}ld_{idx}")
                w = [f"{g}lg_{idx}", f"{g}y_{tb}"]
                if idx >= 2:
                    w.append(f"{g}si_{idx-2}")
                if f == 0 and it > 0:
                    w.append(f"{g}df_{it-1}_3")
                for dc in range(8):
                    ph.emit("tensor", lambda t, f=f, dc=dc: t.matmul(
                        bank(f % 2), wgb[f % 4][:, dc * 128:(dc + 1) * 128],
                        ybuf[:, dc * 512:(dc + 1) * 512], start=(dc == 0), stop=(dc == 7)),
                        waits=(w if dc == 0 else []), ev=(f"{g}g_{idx}" if dc == 7 else None))
                w = [f"{g}lu_{idx}"]
                if idx >= 2:
                    w.append(f"{g}h_{idx-2}")
                if f == 0 and it > 0:
                    w.append(f"{g}df_{it-1}_3")
                for dc in range(8):
                    ph.emit("tensor", lambda t, f=f, dc=dc: t.matmul(
                        bank(2 + f % 2), wub[f % 4][:, dc * 128:(dc + 1) * 128],
                        ybuf[:, dc * 512:(dc + 1) * 512], start=(dc == 0), stop=(dc == 7)),
                        waits=(w if dc == 0 else []), ev=(f"{g}u_{idx}" if dc == 7 else None))
                w = [f"{g}g_{idx}"]
                if idx >= 2:
                    w.append(f"{g}h_{idx-2}")
                ph.emit("scalar", lambda s, f=f: s.activation(slg[f % 2][:], bank(f % 2), AF.Silu),
                        waits=w, ev=f"{g}si_{idx}")
                w = [f"{g}u_{idx}", f"{g}si_{idx}"]
                if f == 0 and it > 0:
                    w.append(f"{g}bD_{it-1}_7")
                ph.emit("vector", lambda v, f=f: v.tensor_mul(
                    hbuf[:, f * 512:(f + 1) * 512], slg[f % 2][:], bank(2 + f % 2)),
                    waits=w, ev=f"{g}h_{idx}")
            for f in range(32):
                idx = it * 32 + f
                order = [4, 5, 6, 7, 0, 1, 2, 3] if f == 0 else list(range(8))
                for oi, dc in enumerate(order):
                    w = []
                    if oi == 0:
                        w = [f"{g}ld_{idx}", f"{g}h_{idx}"]
                    if f == 0 and dc == 4 and it > 0:
                        w.append(f"{g}df_{it-1}_7")
                    if f == 0 and dc == 0:
                        w.append(f"{g}si_{32*it+31}")
                        w.append(f"{g}h_{32*it+31}")
                    last_of_f = (oi == len(order) - 1)
                    ph.emit("tensor", lambda t, f=f, dc=dc: t.matmul(
                        bank(dc), wdb[f % 4][:, dc * 128:(dc + 1) * 128],
                        hbuf[:, f * 512:(f + 1) * 512], start=(f == 0), stop=(f == 31)),
                        waits=w,
                        ev=(f"{g}b_{idx}" if (f < 31 and last_of_f) else
                            (f"{g}bD_{it}_{dc}" if f == 31 else None)))
            for dc in range(8):
                ph.emit("vector", lambda v, dc=dc, tb=tb: v.scalar_tensor_tensor(
                    c.xt[:, dc * T + tb * 512: dc * T + tb * 512 + 512],
                    bank(dc), 1.0,
                    c.xt[:, dc * T + tb * 512: dc * T + tb * 512 + 512],
                    ALU.mult, ALU.add),
                    waits=[f"{g}bD_{it}_{dc}"], ev=f"{g}df_{it}_{dc}")
        ph.seal()
        ph.build()


def ph_l1ffn(c, tag):
    nc = c.nc
    g = tag
    with ExitStack() as ps:
        w1b = [ps.enter_context(nc.sbuf_tensor(f"w1b{i}_{c.uid}", [128, 1024], BF16)) for i in range(4)]
        w2b = [ps.enter_context(nc.sbuf_tensor(f"w2b{i}_{c.uid}", [128, 4096], BF16)) for i in range(2)]
        ybuf = [ps.enter_context(nc.sbuf_tensor(f"yfb0_{c.uid}", [128, 4096], BF16))] * 2
        h1 = ps.enter_context(nc.sbuf_tensor(f"h1buf_{c.uid}", [128, 32 * 512], BF16))
        tmp = [ps.enter_context(nc.sbuf_tensor(f"oftmp{i}_{c.uid}", [128, 512], F32)) for i in range(2)]
        pall = ps.enter_context(nc.psum_tensor(f"l1f_ps_{c.uid}", [128, 4096], F32))

        def bank(b):
            return pall[:, b * 512:(b + 1) * 512]

        ph = new_phase(c)
        for tb in range(8):
            w = [f"{g}hm_{32*(tb-1)+31}"] if tb >= 1 else []
            ph.dma("gpsimd", lambda e, tb=tb: e.dma_start(
                out=ybuf[tb % 2].rearrange("p (dc t) -> p dc t", t=512),
                in_=c.yT[:, tb * 512:(tb + 1) * 512].rearrange("(dc p) t -> p dc t", p=128)),
                waits=w, ev=f"{g}y_{tb}")
            for hc in range(32):
                i = tb * 32 + hc
                w = [f"{g}hm_{i-4}"] if i >= 4 else []
                ph.dma("sync", lambda s, hc=hc: s.dma_start(out=w1b[hc % 4][:], in_=c.w1s[hc]),
                       waits=w, ev=f"{g}l1_{i}")
                w = [f"{g}l1_{i}", f"{g}y_{tb}"]
                if i >= 2:
                    w.append(f"{g}he_{i-2}")
                for dc in range(8):
                    ph.emit("tensor", lambda t, hc=hc, dc=dc, tb=tb: t.matmul(
                        bank(hc % 2), w1b[hc % 4][:, dc * 128:(dc + 1) * 128],
                        ybuf[tb % 2][:, dc * 512:(dc + 1) * 512], start=(dc == 0), stop=(dc == 7)),
                        waits=(w if dc == 0 else []), ev=(f"{g}hm_{i}" if dc == 7 else None))
                w = [f"{g}hm_{i}"]
                if tb > 0 and hc == 0:
                    w.append(f"{g}om_{8*tb-1}")
                ph.emit("scalar", lambda s, hc=hc: s.activation(
                    h1[:, hc * 512:(hc + 1) * 512], bank(hc % 2), AF.Gelu,
                    scale=c.a1[:, hc:hc + 1], bias=c.b1[:, hc:hc + 1]),
                    waits=w, ev=f"{g}he_{i}")
            for cc in range(8):
                j = tb * 8 + cc
                w = [f"{g}om_{j-2}"] if j >= 2 else []
                ph.dma("sync", lambda s, cc=cc: s.dma_start(out=w2b[cc % 2][:], in_=c.w2s[cc]),
                       waits=w, ev=f"{g}l2_{j}")
                w = [f"{g}l2_{j}"]
                if j >= 2:
                    w.append(f"{g}oe_{j-2}")
                for hc in range(32):
                    wh = (w if hc == 0 else []) + [f"{g}he_{32*tb+hc}"]
                    ph.emit("tensor", lambda t, cc=cc, hc=hc: t.matmul(
                        bank(2 + cc % 2), w2b[cc % 2][:, hc * 128:(hc + 1) * 128],
                        h1[:, hc * 512:(hc + 1) * 512], start=(hc == 0), stop=(hc == 31)),
                        waits=wh, ev=(f"{g}om_{j}" if hc == 31 else None))
                w = [f"{g}om_{j}"]
                if j >= 2:
                    w.append(f"{g}ox_{j-2}")
                ph.emit("scalar", lambda s, cc=cc: s.activation(
                    tmp[cc % 2][:], bank(2 + cc % 2), AF.Identity,
                    scale=c.a2h[:, cc:cc + 1], bias=c.b2h[:, cc:cc + 1]),
                    waits=w, ev=f"{g}oe_{j}")
                ph.emit("vector", lambda v, cc=cc, tb=tb: v.tensor_add(
                    c.xt[:, cc * T + tb * 512: cc * T + tb * 512 + 512],
                    c.xt[:, cc * T + tb * 512: cc * T + tb * 512 + 512],
                    tmp[cc % 2][:]),
                    waits=[f"{g}oe_{j}"], ev=f"{g}ox_{j}")
        ph.seal()
        ph.build()


def ph_attn(c, layer, tag):
    nc = c.nc
    g = tag

    def score_tiles(qb):
        out = []
        if layer == 0:
            for kt in range(0, 2 * qb + 2):
                mv = 0 if kt == 2 * qb else (1 if kt == 2 * qb + 1 else None)
                out.append((kt, mv, False))
        else:
            if qb == 0:
                out.append((0, 3, True))
            for kt in (2 * qb - 1, 2 * qb, 2 * qb + 1):
                if kt >= 0:
                    out.append((kt, kt - 2 * qb + 1, False))
        return out

    def av_tiles(qb, qh):
        out = []
        if layer == 0:
            out = [(kt, False) for kt in range(0, 2 * qb + qh + 1)]
        else:
            if qb == 0 and qh == 0:
                out.append((0, True))
            for kt in (2 * qb + qh - 1, 2 * qb + qh):
                if kt >= 0:
                    out.append((kt, False))
        return out

    def slot_of(qb, kt, isinit):
        if layer == 0:
            return kt
        if isinit:
            return 3
        return kt - (2 * qb - 1) if qb > 0 else kt + 1

    SCT = [score_tiles(qb) for qb in range(16)]
    AVT = [[av_tiles(qb, qh) for qh in range(2)] for qb in range(16)]
    nsl = 32 if layer == 0 else 4

    batches = []
    sidx = {}
    si = 0
    tile_batch = {}
    for qb in range(16):
        run = []
        for (kt, mv, isinit) in SCT[qb]:
            sidx[(qb, kt, isinit)] = si
            ent = (si, kt, mv, isinit, slot_of(qb, kt, isinit))
            si += 1
            if isinit:
                batches.append((qb, [ent]))
            else:
                if run and (len(run) == 4 or run[-1][4] + 1 != ent[4]):
                    batches.append((qb, run))
                    run = []
                run.append(ent)
        if run:
            batches.append((qb, run))
    NB = len(batches)
    for bi, (qb, ents) in enumerate(batches):
        for ent in ents:
            tile_batch[ent[0]] = bi
    ns_sc = si

    with ExitStack() as ps:
        if layer == 0:
            yh = ps.enter_context(nc.sbuf_tensor(f"yh_{c.uid}", [64, 4096], BF16))
        else:
            ybf = [ps.enter_context(nc.sbuf_tensor(f"atyb{i}_{c.uid}", [128, 4096], BF16)) for i in range(2)]
            wqb = ps.enter_context(nc.sbuf_tensor(f"wqb_{c.uid}", [128, 512], BF16))
            wkb = ps.enter_context(nc.sbuf_tensor(f"wkb_{c.uid}", [128, 512], BF16))
            wvb = ps.enter_context(nc.sbuf_tensor(f"wvb_{c.uid}", [128, 512], BF16))
            kib = ps.enter_context(nc.sbuf_tensor(f"kib_{c.uid}", [65, 63], BF16))
            vib = ps.enter_context(nc.sbuf_tensor(f"vib_{c.uid}", [63, 65], BF16))
            ksqi = ps.enter_context(nc.sbuf_tensor(f"ksqi_{c.uid}", [64, 63], BF16))
        qa = ps.enter_context(nc.sbuf_tensor(f"qa_{c.uid}", [65, 4096], BF16))
        ka = ps.enter_context(nc.sbuf_tensor(f"ka_{c.uid}", [65, 4096], BF16))
        va = [ps.enter_context(nc.sbuf_tensor(f"va{i}_{c.uid}", [64, 512], BF16)) for i in range(2)]
        vtm = ps.enter_context(nc.sbuf_tensor(f"vtm_{c.uid}", [128, 32 * 65], BF16))
        abuf = ps.enter_context(nc.sbuf_tensor(f"abuf_{c.uid}", [128, nsl * 256], BF16))
        o_h = [ps.enter_context(nc.sbuf_tensor(f"oh{i}_{c.uid}", [64, 4096], BF16)) for i in range(2)]
        wo_b = [ps.enter_context(nc.sbuf_tensor(f"wob{i}_{c.uid}", [64, 1024], BF16)) for i in range(2)]
        otm = [ps.enter_context(nc.sbuf_tensor(f"otm{i}_{c.uid}", [128, 64], BF16)) for i in range(2)]
        ksq = [ps.enter_context(nc.sbuf_tensor(f"ksq{i}_{c.uid}", [64, 512], BF16)) for i in range(2)]
        kmax8 = ps.enter_context(nc.sbuf_tensor(f"kmax8_{c.uid}", [128, 20], F32))
        kmaxv = ps.enter_context(nc.sbuf_tensor(f"kmaxv_{c.uid}", [128, 1], F32))
        qmaxv = ps.enter_context(nc.sbuf_tensor(f"qmaxv_{c.uid}", [128, 1], F32))
        cb2 = ps.enter_context(nc.sbuf_tensor(f"cb2_{c.uid}", [128, 1], F32))
        cb2l = ps.enter_context(nc.sbuf_tensor(f"cb2l_{c.uid}", [128, 1], F32))
        cval = ps.enter_context(nc.sbuf_tensor(f"cval_{c.uid}", [128, 1], F32))
        rcp = ps.enter_context(nc.sbuf_tensor(f"rcp_{c.uid}", [128, 1], F32))
        wtmp = [ps.enter_context(nc.sbuf_tensor(f"wtmp{i}_{c.uid}", [128, 512], F32)) for i in range(2)]
        pf32 = ps.enter_context(nc.psum_tensor(f"at_ps_{c.uid}", [128, 3072], F32))
        pbf = ps.enter_context(nc.psum_tensor(f"at_pb_{c.uid}", [128, 2048], BF16))

        def fb(b):
            return pf32[:, b * 512:(b + 1) * 512]

        def reg(r):
            return pf32[:, r * 1024:(r + 1) * 1024]

        def bb(b):
            return pbf[:, b * 1024:(b + 1) * 1024]

        woT = c.l0_woT_bf if layer == 0 else c.l1_woT_bf
        ph = new_phase(c)

        ph.emit("gpsimd", lambda e: e.memset(ka[64:65, :], -1.0), inc=0)
        ph.emit("gpsimd", lambda e: e.memset(qa[64:65, :], 0.0), inc=0)
        for kc in range(32):
            ph.emit("gpsimd", lambda e, kc=kc: e.memset(vtm[:, kc * 65 + 64: kc * 65 + 65], 1.0),
                    inc=0)
        if layer == 1:
            ph.emit("gpsimd", lambda e: e.memset(kib[64:65, :], -1.0), inc=0)
            ph.emit("gpsimd", lambda e: e.memset(vib[:, 64:65], 1.0), inc=0)

        for h in range(NH):
            hh = h % 2
            p = h // 2
            exlast_prev = f"{g}ex_{h-1}_{NB-1}"
            w = [f"{g}womm_{p-1}_63"] if p > 0 else []
            ph.dma("sync", lambda e, h=h, hh=hh: e.dma_start(
                out=wo_b[hh][:], in_=woT[64 * h:64 * h + 64, :]), waits=w, ev=f"{g}wol_{h}")
            if layer == 0:
                w = [f"{g}pv_{h-1}_7"] if h > 0 else []
                ph.dma("gpsimd", lambda e, h=h: e.dma_start(out=yh[:], in_=c.yT[64 * h:64 * h + 64, :]),
                       waits=w, ev=f"{g}yh_{h}")
            else:
                for wsrc, wdst, wn in ((c.l1_wqT_bf, wqb, "q"), (c.l1_wkT_bf, wkb, "k"),
                                       (c.l1_wvT_bf, wvb, "v")):
                    w = [f"{g}pv_{h-1}_7"] if h > 0 else []
                    ph.dma("sync", lambda e, wsrc=wsrc, wdst=wdst, h=h: e.dma_start(
                        out=wdst.rearrange("p (dc j) -> p dc j", j=64),
                        in_=wsrc[:, 64 * h:64 * h + 64].rearrange("(dc p) j -> p dc j", p=128)),
                        waits=w, ev=f"{g}wl{wn}_{h}")
                w = [f"{g}ksqi_{h-1}", f"{g}sm_{h-1}_0"] if h > 0 else []
                ph.dma("gpsimd", lambda e, h=h: e.dma_start(out=kib[0:64, :], in_=c.ki_bf[h]),
                       waits=w, ev=f"{g}kil_{h}")
                w = [f"{g}avg_{h-1}_0"] if h > 0 else []
                ph.dma("gpsimd", lambda e, h=h: e.dma_start(out=vib[0:63, 0:64], in_=c.vi_bf[h]),
                       waits=w, ev=f"{g}vil_{h}")

            for tb in range(8):
                i = 8 * h + tb
                if layer == 1:
                    w = [f"{g}pv_{(i-2)//8}_{(i-2)%8}"] if i >= 2 else []
                    ph.dma("gpsimd", lambda e, tb=tb: e.dma_start(
                        out=ybf[tb % 2].rearrange("p (dc t) -> p dc t", t=512),
                        in_=c.yT[:, tb * 512:(tb + 1) * 512].rearrange("(dc p) t -> p dc t", p=128)),
                        waits=w, ev=f"{g}yl_{h}_{tb}")
                for (bnk, wn, wtile, pev, bev) in (
                        (0, "q", None, "pq", "eq"), (1, "k", None, "pk", "ek"), (2, "v", None, "pv", "ev")):
                    w = []
                    if layer == 0:
                        w.append(f"{g}yh_{h}")
                        lhs = {"q": c.l0wq, "k": c.l0wk, "v": c.l0wv}[wn]
                    else:
                        w.append(f"{g}yl_{h}_{tb}")
                        w.append(f"{g}wl{wn}_{h}")
                        lhs = {"q": wqb, "k": wkb, "v": wvb}[wn]
                    if tb > 0:
                        w.append(f"{g}{bev}_{h}_{tb-1}")
                    elif h > 0:
                        if hh == 0:
                            w.append(f"{g}woev_{p-1}_62")
                            w.append(f"{g}wog_{p-1}_63")
                        else:
                            w.append(exlast_prev)
                    if layer == 0:
                        ph.emit("tensor", lambda t, tb=tb, bnk=bnk, lhs=lhs: t.matmul(
                            fb(bnk)[0:64, :], lhs[:], yh[:, tb * 512:(tb + 1) * 512],
                            start=True, stop=True), waits=w, ev=f"{g}{pev}_{h}_{tb}")
                    else:
                        for dc in range(8):
                            ph.emit("tensor", lambda t, tb=tb, dc=dc, bnk=bnk, lhs=lhs: t.matmul(
                                fb(bnk)[0:64, :], lhs[:, dc * 64:(dc + 1) * 64],
                                ybf[tb % 2][:, dc * 512:(dc + 1) * 512],
                                start=(dc == 0), stop=(dc == 7)),
                                waits=(w if dc == 0 else []),
                                ev=(f"{g}{pev}_{h}_{tb}" if dc == 7 else None))
                w = [f"{g}pq_{h}_{tb}"]
                if tb == 0 and h > 0:
                    w.append(f"{g}avg_{h-1}_31")
                ph.emit("vector", lambda v, h=h, tb=tb: v.tensor_scalar_mul(
                    qa[0:64, tb * 512:(tb + 1) * 512], fb(0)[0:64, :], 0.125),
                    waits=w, ev=f"{g}eq_{h}_{tb}")
                w = [f"{g}pk_{h}_{tb}"]
                if tb == 0 and h > 0:
                    w.append(f"{g}avg_{h-1}_31")
                ph.emit("vector", lambda v, h=h, tb=tb: v.tensor_copy(
                    ka[0:64, tb * 512:(tb + 1) * 512], fb(1)[0:64, :]),
                    waits=w, ev=f"{g}ek_{h}_{tb}")
                w = [f"{g}pv_{h}_{tb}"]
                if tb >= 2:
                    w.append(f"{g}vt_{h}_{tb-2}_3")
                elif h > 0:
                    w.append(f"{g}vt_{h-1}_{tb+6}_3")
                ph.emit("scalar", lambda s, tb=tb: s.copy(va[tb % 2][0:64, :], fb(2)[0:64, :]),
                        waits=w, ev=f"{g}ev_{h}_{tb}")
                w = [f"{g}ek_{h}_{tb}"]
                if tb >= 2:
                    w.append(f"{g}kn_{h}_{tb-2}")
                elif h > 0:
                    w.append(f"{g}qn_{h-1}_{tb+6}")
                ph.emit("vector", lambda v, tb=tb: v.tensor_mul(
                    ksq[tb % 2][0:64, :], ka[0:64, tb * 512:(tb + 1) * 512],
                    ka[0:64, tb * 512:(tb + 1) * 512]),
                    waits=w, ev=f"{g}ksq_{h}_{tb}")
                w = [f"{g}ksq_{h}_{tb}"]
                if tb > 0:
                    w.append(f"{g}knr_{h}_{tb-1}")
                elif h > 0:
                    if hh == 0:
                        w.append(f"{g}wog_{p-1}_63")
                    else:
                        w.append(exlast_prev)
                ph.emit("tensor", lambda t, tb=tb: t.matmul(
                    fb(3), c.ones[0:64, :], ksq[tb % 2][0:64, :], start=True, stop=True),
                    waits=w, ev=f"{g}kn_{h}_{tb}")
                ph.emit("vector", lambda v, tb=tb: v.reduce_max(kmax8[:, tb:tb + 1], fb(3), axis=AX.X),
                        waits=[f"{g}kn_{h}_{tb}"], ev=f"{g}knr_{h}_{tb}")
                for j in range(4):
                    kc = 4 * tb + j
                    w = [f"{g}ev_{h}_{tb}"]
                    if kc >= 2:
                        w.append(f"{g}vau_{h}_{kc-2}")
                    elif h > 0:
                        w.append(f"{g}oev_{h-1}_31")
                    ph.emit("tensor", lambda t, tb=tb, j=j, kc=kc: t.transpose(
                        bb(kc % 2)[:, 0:64], va[tb % 2][0:64, j * 128:(j + 1) * 128],
                        c.ident[0:64, 0:64]),
                        waits=w, ev=f"{g}vt_{h}_{tb}_{j}")
                    w = [f"{g}vt_{h}_{tb}_{j}"]
                    if h > 0 and kc == 0:
                        w.append(f"{g}avg_{h-1}_31")
                    ph.emit("scalar", lambda v, kc=kc: v.copy(
                        vtm[:, kc * 65: kc * 65 + 64], bb(kc % 2)[:, 0:64]),
                        waits=w, ev=f"{g}vau_{h}_{kc}")
            if layer == 1:
                ph.emit("scalar", lambda s: s.activation(ksqi[0:64, :], kib[0:64, :], AF.Square),
                        waits=[f"{g}kil_{h}"], ev=f"{g}ksqi_{h}")
                ph.emit("tensor", lambda t: t.matmul(
                    fb(3)[:, 0:63], c.ones[0:64, :], ksqi[0:64, :], start=True, stop=True),
                    waits=[f"{g}ksqi_{h}", f"{g}knr_{h}_7"], ev=f"{g}kni_{h}")
                ph.emit("vector", lambda v: v.reduce_max(kmax8[:, 8:9], fb(3)[:, 0:63], axis=AX.X),
                        waits=[f"{g}kni_{h}"], ev=f"{g}knri_{h}")
            ncol = 8 if layer == 0 else 9
            w = [f"{g}knr_{h}_7"] + ([f"{g}knri_{h}"] if layer == 1 else [])
            ph.emit("vector", lambda v, ncol=ncol: v.reduce_max(kmaxv[:], kmax8[:, 0:ncol], axis=AX.X),
                    waits=w, ev=f"{g}kmx_{h}")

            for tb in range(8):
                w = [f"{g}eq_{h}_{tb}",
                     f"{g}kn_{h}_7" if layer == 0 else f"{g}kni_{h}"]
                if tb >= 2:
                    w.append(f"{g}qn_{h}_{tb-2}")
                ph.emit("vector", lambda v, tb=tb: v.tensor_mul(
                    ksq[tb % 2][0:64, :], qa[0:64, tb * 512:(tb + 1) * 512],
                    qa[0:64, tb * 512:(tb + 1) * 512]),
                    waits=w, ev=f"{g}qsq_{h}_{tb}")
                w = [f"{g}qsq_{h}_{tb}"]
                if tb == 0:
                    w.append(f"{g}eq_{h}_7")
                elif tb == 1:
                    w.append(f"{g}ek_{h}_7")
                elif tb == 2:
                    w.append(f"{g}ev_{h}_7")
                elif tb == 3:
                    w.append(f"{g}knri_{h}" if layer == 1 else f"{g}knr_{h}_7")
                else:
                    w.append(f"{g}qred_{h}_0")
                ph.emit("tensor", lambda t, tb=tb: t.matmul(
                    fb(tb % 4), c.ones[0:64, :], ksq[tb % 2][0:64, :], start=True, stop=True),
                    waits=w, ev=f"{g}qn_{h}_{tb}")
                if tb % 4 == 3:
                    ph.emit("vector", lambda v, tb=tb: v.reduce_max(
                        kmax8[:, 10 + tb // 4:11 + tb // 4], pf32[:, 0:2048], axis=AX.X),
                        waits=[f"{g}qn_{h}_{tb}"], ev=f"{g}qred_{h}_{tb//4}")
            ph.emit("vector", lambda v: v.reduce_max(qmaxv[:], kmax8[:, 10:12], axis=AX.X),
                    waits=[f"{g}qred_{h}_1"], ev=f"{g}qmx_{h}")
            ph.emit("vector", lambda v: v.tensor_mul(cb2[:], qmaxv[:], kmaxv[:]),
                    waits=[f"{g}qmx_{h}", f"{g}kmx_{h}"], ev=f"{g}cb2_{h}")
            ph.emit("scalar", lambda s: s.activation(cb2l[:], cb2[:], AF.Ln, bias=c.tiny[:]),
                    waits=[f"{g}cb2_{h}"], ev=f"{g}cln_{h}")
            ph.emit("scalar", lambda s: s.activation(cval[:], cb2l[:], AF.Exp, scale=0.5),
                    waits=[f"{g}cln_{h}"], ev=f"{g}cex_{h}")
            w = [f"{g}cex_{h}"]
            if h > 0:
                w.append(f"{g}avg_{h-1}_31")
            ph.emit("scalar", lambda s: s.activation(
                qa[64:65, :], qa[64:65, :], AF.Identity, scale=0.0, bias=cval[64:65, 0:1]),
                waits=w, ev=f"{g}c_{h}")

            for bi, (qb, ents) in enumerate(batches):
                r = bi % 2
                nb = len(ents)
                last_si = ents[-1][0]
                for z, (si_, kt, mv, isinit, slot) in enumerate(ents):
                    w = []
                    if z == 0:
                        w = [f"{g}c_{h}", f"{g}ek_{h}_7"]
                        if bi >= 2:
                            w.append(f"{g}ex_{h}_{bi-2}")
                        else:
                            w.append(f"{g}qred_{h}_1")
                    if isinit:
                        w = w + [f"{g}kil_{h}"]
                        ph.emit("tensor", lambda t, qb=qb, r=r, z=z: t.matmul(
                            reg(r)[0:63, z * 256:(z + 1) * 256], kib[:, 0:63],
                            qa[:, qb * 256:(qb + 1) * 256], start=True, stop=True),
                            waits=w, ev=f"{g}sm_{h}_{si_}")
                    else:
                        ph.emit("tensor", lambda t, qb=qb, kt=kt, r=r, z=z: t.matmul(
                            reg(r)[:, z * 256:(z + 1) * 256], ka[:, kt * 128:(kt + 1) * 128],
                            qa[:, qb * 256:(qb + 1) * 256], start=True, stop=True),
                            waits=w, ev=f"{g}sm_{h}_{si_}")
                nmk = sum(1 for e2 in ents if e2[2] is not None)
                if nmk:
                    assert ents[-1][2] is not None
                for z, (si_, kt, mv, isinit, slot) in enumerate(ents):
                    if mv is None:
                        continue
                    rows = 64 if isinit else 128
                    if isinit:
                        msk_ap = c.l1mi[:]
                    elif layer == 0:
                        msk_ap = c.l0m[:, mv * 256:(mv + 1) * 256]
                    else:
                        msk_ap = c.l1m[:, mv * 256:(mv + 1) * 256]
                    ph.emit("vector", lambda v, r=r, z=z, rows=rows, msk_ap=msk_ap: v.tensor_add(
                        reg(r)[0:rows, z * 256:(z + 1) * 256],
                        reg(r)[0:rows, z * 256:(z + 1) * 256], msk_ap),
                        waits=[f"{g}sm_{h}_{last_si}"],
                        ev=(f"{g}mk_{h}_{bi}" if z == len(ents) - 1 else None))
                w = [f"{g}sm_{h}_{last_si}"]
                if nmk:
                    w.append(f"{g}mk_{h}_{bi}")
                if qb > 0:
                    w.append(f"{g}avg_{h}_{2*qb-1}")
                elif h > 0:
                    w.append(f"{g}avg_{h-1}_31")
                rows = 64 if ents[0][3] else 128
                slot0 = ents[0][4]
                ph.emit("scalar", lambda s, r=r, nb=nb, rows=rows, slot0=slot0: s.activation(
                    abuf[0:rows, slot0 * 256:(slot0 + nb) * 256],
                    reg(r)[0:rows, 0:nb * 256], AF.Exp),
                    waits=w, ev=f"{g}ex_{h}_{bi}")
                if bi + 1 < len(batches) and batches[bi + 1][0] == qb:
                    continue
                for qh in range(2):
                    a = 2 * qb + qh
                    tiles = AVT[qb][qh]
                    w = sorted({f"{g}ex_{h}_{tile_batch[sidx[(qb, kt, ii)]]}" for (kt, ii) in tiles})
                    if a >= 2:
                        w.append(f"{g}otm_{h}_{a-2}")
                    elif h > 0:
                        w.append(f"{g}otm_{h-1}_{30+a}")
                    for z, (kt, ii) in enumerate(tiles):
                        slot = slot_of(qb, kt, ii)
                        lhs_rows = 63 if ii else 128
                        ph.emit("tensor", lambda t, qh=qh, kt=kt, ii=ii, slot=slot,
                                z=z, a=a, lhs_rows=lhs_rows, nt=len(tiles): t.matmul(
                            fb(4 + a % 2)[:, 0:65],
                            abuf[0:lhs_rows, slot * 256 + qh * 128: slot * 256 + qh * 128 + 128],
                            (vib[0:63, :] if ii else vtm[:, kt * 65:(kt + 1) * 65]),
                            start=(z == 0), stop=(z == nt - 1)),
                            waits=(w if z == 0 else []),
                            ev=(f"{g}avg_{h}_{a}" if z == len(tiles) - 1 else None))
                    ph.emit("vector", lambda v, a=a: v.reciprocal(rcp[:], fb(4 + a % 2)[:, 64:65]),
                            waits=[f"{g}avg_{h}_{a}"], ev=f"{g}rc_{h}_{a}")
                    w = [f"{g}rc_{h}_{a}"]
                    if a >= 2:
                        w.append(f"{g}otr_{h}_{a-2}")
                    elif h > 0:
                        w.append(f"{g}otr_{h-1}_{30+a}")
                    ph.emit("scalar", lambda v, a=a: v.activation(
                        otm[a % 2][:, 0:64], fb(4 + a % 2)[:, 0:64], AF.Identity,
                        scale=rcp[:]),
                        waits=w, ev=f"{g}otm_{h}_{a}")
                    w = [f"{g}otm_{h}_{a}", f"{g}vau_{h}_31"]
                    if a >= 2:
                        w.append(f"{g}oev_{h}_{a-2}")
                    ph.emit("tensor", lambda t, a=a: t.transpose(
                        bb(a % 2)[0:64, 128:256], otm[a % 2][:, 0:64], c.ident[:]),
                        waits=w, ev=f"{g}otr_{h}_{a}")
                    w = [f"{g}otr_{h}_{a}"]
                    if h >= 2 and a == 0:
                        w.append(f"{g}womm_{p-1}_63")
                    ph.emit("vector", lambda v, a=a, qb=qb, qh=qh, hh=hh: v.tensor_copy(
                        o_h[hh][0:64, qb * 256 + qh * 128: qb * 256 + qh * 128 + 128],
                        bb(a % 2)[0:64, 128:256]),
                        waits=w, ev=f"{g}oev_{h}_{a}")

            if hh == 1:
                for j in range(64):
                    dc, tb = j // 8, j % 8
                    w = [f"{g}wol_{2*p}", f"{g}wol_{2*p+1}",
                         f"{g}oev_{2*p}_31", f"{g}oev_{2*p+1}_31"]
                    if j >= 4:
                        jj = j - 4
                        w.append(f"{g}woev_{p}_{jj}" if jj % 2 == 0 else f"{g}wog_{p}_{jj}")
                    else:
                        w.append(f"{g}ex_{h}_{NB-1}")
                    ph.emit("tensor", lambda t, dc=dc, tb=tb, j=j: t.matmul(
                        fb(j % 4), wo_b[0][:, dc * 128:(dc + 1) * 128],
                        o_h[0][0:64, tb * 512:(tb + 1) * 512], start=True, stop=False),
                        waits=w, ev=None)
                    ph.emit("tensor", lambda t, dc=dc, tb=tb, j=j: t.matmul(
                        fb(j % 4), wo_b[1][:, dc * 128:(dc + 1) * 128],
                        o_h[1][0:64, tb * 512:(tb + 1) * 512], start=False, stop=True),
                        waits=[], ev=f"{g}womm_{p}_{j}")
                    if j % 2 == 0:
                        ph.emit("vector", lambda v, dc=dc, tb=tb, j=j: v.tensor_add(
                            c.xt[:, dc * T + tb * 512: dc * T + tb * 512 + 512],
                            c.xt[:, dc * T + tb * 512: dc * T + tb * 512 + 512],
                            fb(j % 4)),
                            waits=[f"{g}womm_{p}_{j}"], ev=f"{g}woev_{p}_{j}")
                    else:
                        wv = [f"{g}womm_{p}_{j}"]
                        if j >= 4:
                            wv.append(f"{g}wog_{p}_{j-4}")
                        elif p > 0:
                            wv.append(f"{g}wog_{p-1}_{60+j}")
                        ph.emit("scalar", lambda sc, j=j: sc.copy(
                            wtmp[(j // 2) % 2][:], fb(j % 4)),
                            waits=wv, ev=f"{g}wcp_{p}_{j}")
                        ph.emit("gpsimd", lambda e, dc=dc, tb=tb, j=j: e.tensor_add(
                            c.xt[:, dc * T + tb * 512: dc * T + tb * 512 + 512],
                            c.xt[:, dc * T + tb * 512: dc * T + tb * 512 + 512],
                            wtmp[(j // 2) % 2][:]),
                            waits=[f"{g}wcp_{p}_{j}"], ev=f"{g}wog_{p}_{j}", sem="gc")
        ph.seal()
        ph.build()



def build_program(phases, debug_yT=False):
    nc = bass.Bass("TRN2", target_bir_lowering=False)
    c = C()
    c.nc = nc
    c.uid = 0
    needed = set(ALWAYS_INPUTS)
    for p in phases:
        needed.update(PHASE_INPUTS[p])

    c.xT = nc.dram_tensor("xT", [D, T], F32, kind="ExternalInput")
    c.outT = nc.dram_tensor("outT", [D, T], F32, kind="ExternalOutput")
    c.dram = {}
    for name, shape, dt in COMMON_SPECS:
        if name in needed:
            c.dram[name] = nc.dram_tensor(name, shape, dt, kind="ExternalInput")

    if debug_yT:
        c.yT = nc.dram_tensor("yT", [D, T], BF16, kind="ExternalOutput")
    else:
        c.yT = nc.dram_tensor("yT", [D, T], BF16)
    if "pre_l0ffn" in phases:
        c.wg_bf = nc.dram_tensor("wg_bf", [NE, 32, 128, 1024], BF16)
        c.wu_bf = nc.dram_tensor("wu_bf", [NE, 32, 128, 1024], BF16)
        c.wd_bf = nc.dram_tensor("wd_bf", [NE, FF, D], BF16)
    if "pre_l0attn" in phases:
        c.l0_woT_bf = nc.dram_tensor("l0_woT_bf", [D, D], BF16)
    if "pre_l1attn" in phases:
        c.l1_wqT_bf = nc.dram_tensor("l1_wqT_bf", [D, D], BF16)
        c.l1_wkT_bf = nc.dram_tensor("l1_wkT_bf", [D, D], BF16)
        c.l1_wvT_bf = nc.dram_tensor("l1_wvT_bf", [D, D], BF16)
        c.l1_woT_bf = nc.dram_tensor("l1_woT_bf", [D, D], BF16)
        c.ki_bf = nc.dram_tensor("ki_bf", [NH, 64, 63], BF16)
        c.vi_bf = nc.dram_tensor("vi_bf", [NH, 63, 64], BF16)
    if "pre_l1ffn" in phases:
        c.w1s = nc.dram_tensor("w1s", [32, 128, 1024], BF16)
        c.w2s = nc.dram_tensor("w2s", [8, 128, 4096], BF16)
        c.alpha1d = nc.dram_tensor("alpha1d", [32, 128, 1], F32)
        c.alpha2d = nc.dram_tensor("alpha2d", [8, 128, 1], F32)

    with ExitStack() as ctx:
        c.xt = ctx.enter_context(nc.sbuf_tensor(f"xt_{c.uid}", [128, 8 * T], F32))
        c.ones = ctx.enter_context(nc.sbuf_tensor(f"c_ones_{c.uid}", [128, 128], BF16))
        c.ident = ctx.enter_context(nc.sbuf_tensor(f"c_ident_{c.uid}", [128, 128], BF16))
        c.eps = ctx.enter_context(nc.sbuf_tensor(f"c_eps_{c.uid}", [128, 1], F32))
        c.tiny = ctx.enter_context(nc.sbuf_tensor(f"c_tiny_{c.uid}", [128, 1], F32))
        if "pre_l0attn" in phases:
            c.l0wq = ctx.enter_context(nc.sbuf_tensor(f"l0wq_{c.uid}", [64, 64], BF16))
            c.l0wk = ctx.enter_context(nc.sbuf_tensor(f"l0wk_{c.uid}", [64, 64], BF16))
            c.l0wv = ctx.enter_context(nc.sbuf_tensor(f"l0wv_{c.uid}", [64, 64], BF16))
        if "l0attn" in phases:
            c.l0m = ctx.enter_context(nc.sbuf_tensor(f"l0m_{c.uid}", [128, 512], F32))
        if "l1attn" in phases:
            c.l1m = ctx.enter_context(nc.sbuf_tensor(f"l1m_{c.uid}", [128, 768], F32))
            c.l1mi = ctx.enter_context(nc.sbuf_tensor(f"l1mi_{c.uid}", [64, 256], F32))
        if "pre_l1ffn" in phases:
            c.a1 = ctx.enter_context(nc.sbuf_tensor(f"a1_{c.uid}", [128, 32], F32))
            c.b1 = ctx.enter_context(nc.sbuf_tensor(f"b1_{c.uid}", [128, 32], F32))
            c.a2h = ctx.enter_context(nc.sbuf_tensor(f"a2h_{c.uid}", [128, 8], F32))
            c.b2h = ctx.enter_context(nc.sbuf_tensor(f"b2h_{c.uid}", [128, 8], F32))

        c.semh = {}
        c.semn = {s: 0 for s in SEMS}
        c.sems = {}
        for s in SEMS:
            c.semh[s] = ctx.enter_context(nc.semaphore("sem_" + s))

        ph_init(c, needed)
        counters = {}
        for p in phases:
            counters[p] = counters.get(p, 0) + 1
            tag = f"{p}{counters[p]}_"
            if p == "ln":
                ph_ln(c, tag)
            elif p == "pre_l0ffn":
                ph_pre_l0ffn(c)
            elif p == "pre_l0attn":
                ph_pre_small(c, "l0attn")
            elif p == "pre_l1attn":
                ph_pre_small(c, "l1attn")
            elif p == "pre_l1ffn":
                ph_pre_l1ffn(c)
            elif p == "l0ffn":
                ph_l0ffn(c, tag)
            elif p == "l0attn":
                ph_attn(c, 0, tag)
            elif p == "l1attn":
                ph_attn(c, 1, tag)
            elif p == "l1ffn":
                ph_l1ffn(c, tag)
            else:
                raise ValueError(p)
        ph_store(c)

    input_names = ["xT"] + sorted(c.dram.keys())
    return nc, input_names



def _run(nc, input_names, common, xTs):
    in_maps = []
    for cid in range(NCORE):
        m = {}
        for n in input_names:
            m[n] = xTs[cid] if n == "xT" else common[n]
        in_maps.append(m)
    res = run_bass_kernel_spmd(nc, in_maps, core_ids=list(range(NCORE)))
    return [np.asarray(r["outT"]) for r in res.results]


LAUNCHES = [
    ["pre_l0ffn", "ln", "l0ffn"],
    ["pre_l0attn", "ln", "l0attn"],
    ["pre_l0ffn", "ln", "l0ffn"],
    ["pre_l1ffn", "ln", "l1ffn"],
    ["pre_l1attn", "ln", "l1attn"],
    ["pre_l1ffn", "ln", "l1ffn"],
]

FUSED_PHASES = [
    "pre_l0ffn", "pre_l0attn", "pre_l1attn", "pre_l1ffn",
    "ln", "l0ffn",
    "ln", "l0attn",
    "ln", "l0ffn",
    "ln", "l1ffn",
    "ln", "l1attn",
    "ln", "l1ffn",
]

FUSED = True


def kernel(**inputs):
    common = prep_common(inputs)
    x = np.asarray(inputs["L0_attn_x"], dtype=np.float32)
    xTs = [np.ascontiguousarray(x[b].T) for b in range(NCORE)]
    if FUSED:
        nc, names = build_program(list(FUSED_PHASES))
        cur = _run(nc, names, common, xTs)
    else:
        progs = {}
        for ph in LAUNCHES:
            key = tuple(ph)
            if key not in progs:
                progs[key] = build_program(list(ph))
        cur = xTs
        for ph in LAUNCHES:
            nc, names = progs[tuple(ph)]
            cur = _run(nc, names, common, cur)
    out = np.stack([np.ascontiguousarray(np.asarray(t, np.float32).T) for t in cur])
    return np.ascontiguousarray(out)


# version 08 (final)
# speedup vs baseline: 1.0538x; 1.0538x over previous
import numpy as np
import ml_dtypes
from contextlib import ExitStack

import concourse.bass as bass
import concourse.mybir as mybir
from concourse.bass_utils import run_bass_kernel_spmd

F32 = mybir.dt.float32
BF16 = mybir.dt.bfloat16
AF = mybir.ActivationFunctionType
ALU = mybir.AluOpType
AX = mybir.AxisListType
BFNP = ml_dtypes.bfloat16

D = 1024
T = 4096
NCORE = 8
NE = 8
FF = 4096
NH = 16
DH = 64
NEG = -1e30

ENGS = ("sync", "gpsimd", "tensor", "scalar", "vector")
SEMS = ("q", "g", "t", "s", "v", "gc")
ENG_SEM = {"sync": "q", "gpsimd": "g", "tensor": "t", "scalar": "s", "vector": "v"}


class C:
    pass


class Ph:

    def __init__(self, c):
        self.c = c
        self.items = {e: [] for e in ENGS}
        self.ev = {}

    def emit(self, eng, fn, waits=(), ev=None, inc=None, sem=None):
        self.items[eng].append((fn, tuple(waits), ev, inc, sem))

    def dma(self, eng, fn, waits=(), ev=None):
        self.emit(eng, fn, waits, ev, inc=16)

    def seal(self):
        for eng in ENGS:
            for (fn, waits, ev, inc, semo) in self.items[eng]:
                if fn is None:
                    continue
                sem = semo or ENG_SEM[eng]
                amt = inc if inc is not None else 1
                if amt == 0:
                    assert ev is None
                    continue
                self.c.semn[sem] += amt
                if ev is not None:
                    assert ev not in self.ev, f"duplicate event {ev}"
                    self.ev[ev] = (sem, self.c.semn[sem])

    def check(self):
        ptr = {e: 0 for e in ENGS}
        cnt = dict(self.base)
        progress = True
        while progress:
            progress = False
            for eng in ENGS:
                while ptr[eng] < len(self.items[eng]):
                    fn, waits, ev, inc, semo = self.items[eng][ptr[eng]]
                    ok = all(cnt[self.ev[w][0]] >= self.ev[w][1] for w in waits)
                    if not ok:
                        break
                    if fn is not None:
                        amt = inc if inc is not None else 1
                        cnt[semo or ENG_SEM[eng]] += amt
                    ptr[eng] += 1
                    progress = True
        stuck = {e: ptr[e] for e in ENGS if ptr[e] < len(self.items[e])}
        if stuck:
            for e, p in stuck.items():
                fn, waits, ev, inc = self.items[e][p]
                bad = [w for w in waits
                       if cnt[self.ev[w][0]] < self.ev[w][1]]
                raise RuntimeError(f"deadlock: engine {e} item {p} ev={ev} "
                                   f"blocked on {bad}")

    def build(self):
        for w_list in (w for e in ENGS for (_, w, _, _, _) in self.items[e]):
            for w in w_list:
                assert w in self.ev, f"unknown event {w}"
        self.check()
        c = self.c
        with c.nc.Block() as block:
            for eng in ENGS:
                if not self.items[eng]:
                    continue

                def body(engine, eng=eng):
                    maxw = {}
                    for (fn, waits, ev, inc, semo) in self.items[eng]:
                        tgt = {}
                        for w in waits:
                            s, vv = self.ev[w]
                            tgt[s] = max(tgt.get(s, 0), vv)
                        for s, vv in tgt.items():
                            if vv > maxw.get(s, 0):
                                engine.wait_ge(c.semh[s], vv)
                                maxw[s] = vv
                        if fn is None:
                            continue
                        ins = fn(engine)
                        amt = inc if inc is not None else 1
                        if amt:
                            ins.then_inc(c.semh[semo or ENG_SEM[eng]], amt)
                getattr(block, eng)(body)


def new_phase(c):
    ph = Ph(c)
    ph.base = dict(c.semn)
    c.uid = getattr(c, "uid", 0) + 1
    return ph



def _causal_mask_l0(off):
    i = np.arange(128)[:, None]
    j = np.arange(256)[None, :]
    return np.where(off + i <= j, 0.0, NEG).astype(np.float32)


def _win_mask_l1(off):
    i = np.arange(128)[:, None]
    j = np.arange(256)[None, :]
    d = off + i - j
    return np.where((d <= 0) & (d >= -63), 0.0, NEG).astype(np.float32)


def _init_mask_l1():
    i = np.arange(64)[:, None]
    j = np.arange(256)[None, :]
    m = np.where(j <= i, 0.0, NEG).astype(np.float32)
    m[63, :] = NEG
    return m


def prep_common(inputs):
    f = lambda a: np.ascontiguousarray(np.asarray(a), dtype=np.float32)
    c = {}
    c["l0_wq"] = f(np.asarray(inputs["L0_attn_Wq"]).T)
    c["l0_wk"] = f(np.asarray(inputs["L0_attn_Wk"]).T)
    c["l0_wv"] = f(np.asarray(inputs["L0_attn_Wv"]).T)
    c["l0_woT"] = f(np.asarray(inputs["L0_attn_Wo"]).T)
    c["l0_wg"] = f(inputs["L0_ffn_Wg"])
    c["l0_wu"] = f(inputs["L0_ffn_Wu"])
    c["l0_wd"] = f(inputs["L0_ffn_Wd"])
    c["l1_wqT"] = f(np.asarray(inputs["L1_attn_Wq"]).T)
    c["l1_wkT"] = f(np.asarray(inputs["L1_attn_Wk"]).T)
    c["l1_wvT"] = f(np.asarray(inputs["L1_attn_Wv"]).T)
    c["l1_woT"] = f(np.asarray(inputs["L1_attn_Wo"]).T)
    ki = np.asarray(inputs["L1_attn_last_k_init"])
    vi = np.asarray(inputs["L1_attn_last_v_init"])
    c["l1_ki"] = f(np.transpose(ki, (1, 2, 0)))
    c["l1_vi"] = f(np.transpose(vi, (1, 0, 2)))
    c["l1_w1T"] = f(np.asarray(inputs["L1_ffn_w1"]).T)
    c["l1_w2T"] = f(np.asarray(inputs["L1_ffn_w2"]).T)
    c["l1_b1"] = f(np.asarray(inputs["L1_ffn_b1"]).reshape(32, 128, 1))
    c["l1_b2"] = f(np.asarray(inputs["L1_ffn_b2"]).reshape(8, 128, 1))
    c["ones"] = np.ones((128, 128), dtype=BFNP)
    c["ident"] = np.eye(128, dtype=BFNP)
    c["l0_mask"] = np.stack([_causal_mask_l0(0), _causal_mask_l0(128)])
    c["l1_mask"] = np.stack([_win_mask_l1(-128), _win_mask_l1(0), _win_mask_l1(128)])
    c["l1_maski"] = _init_mask_l1()
    return c


COMMON_SPECS = [
    ("l0_wq", [64, 64], F32), ("l0_wk", [64, 64], F32), ("l0_wv", [64, 64], F32),
    ("l0_woT", [D, D], F32),
    ("l0_wg", [NE, D, FF], F32), ("l0_wu", [NE, D, FF], F32), ("l0_wd", [NE, FF, D], F32),
    ("l1_wqT", [D, D], F32), ("l1_wkT", [D, D], F32), ("l1_wvT", [D, D], F32), ("l1_woT", [D, D], F32),
    ("l1_ki", [NH, 64, 63], F32), ("l1_vi", [NH, 63, 64], F32),
    ("l1_w1T", [D, FF], F32), ("l1_w2T", [FF, D], F32),
    ("l1_b1", [32, 128, 1], F32), ("l1_b2", [8, 128, 1], F32),
    ("ones", [128, 128], BF16), ("ident", [128, 128], BF16),
    ("l0_mask", [2, 128, 256], F32), ("l1_mask", [3, 128, 256], F32),
    ("l1_maski", [64, 256], F32),
]

PHASE_INPUTS = {
    "ln": [],
    "pre_l0ffn": ["l0_wg", "l0_wu", "l0_wd"],
    "pre_l0attn": ["l0_wq", "l0_wk", "l0_wv", "l0_woT"],
    "pre_l1attn": ["l1_wqT", "l1_wkT", "l1_wvT", "l1_woT", "l1_ki", "l1_vi"],
    "pre_l1ffn": ["l1_w1T", "l1_w2T", "l1_b1", "l1_b2"],
    "l0ffn": [], "l0attn": ["l0_mask"], "l1attn": ["l1_mask", "l1_maski"], "l1ffn": [],
}
ALWAYS_INPUTS = ["ones", "ident"]



def ph_init(c, needed):
    ph = new_phase(c)
    loads = [(c.xt.rearrange("p (dc t) -> p dc t", t=T),
              c.xT.rearrange("(dc p) t -> p dc t", p=128))]
    loads.append((c.ones[:], c.dram["ones"][:]))
    loads.append((c.ident[:], c.dram["ident"][:]))
    if "l0_mask" in c.dram:
        loads.append((c.l0m.rearrange("p (vv j) -> p vv j", j=256),
                      c.dram["l0_mask"].rearrange("vv p j -> p vv j")))
    if "l1_mask" in c.dram:
        loads.append((c.l1m.rearrange("p (vv j) -> p vv j", j=256),
                      c.dram["l1_mask"].rearrange("vv p j -> p vv j")))
        loads.append((c.l1mi[:], c.dram["l1_maski"][:]))
    for i, (dst, src) in enumerate(loads):
        ph.dma("sync", lambda e, dst=dst, src=src: e.dma_start(out=dst, in_=src),
               ev=f"ini_{i}")
    ph.emit("sync", None, waits=[f"ini_{len(loads)-1}"])
    ph.emit("vector", lambda e: e.memset(c.eps[:], 1e-5), ev="ini_eps")
    ph.emit("vector", lambda e: e.memset(c.tiny[:], 1e-30), ev="ini_tiny")
    ph.seal()
    ph.build()


def ph_store(c):
    ph = new_phase(c)
    ph.dma("sync", lambda e: e.dma_start(
        out=c.outT.rearrange("(dc p) t -> p dc t", p=128),
        in_=c.xt.rearrange("p (dc t) -> p dc t", t=T)), ev="sto")
    ph.emit("sync", None, waits=["sto"])
    ph.seal()
    ph.build()


def ph_pre_l0ffn(c):
    nc = c.nc
    items = ([("wg", e, i) for e in range(NE) for i in range(8)]
             + [("wu", e, i) for e in range(NE) for i in range(8)]
             + [("wd", e, i) for e in range(NE) for i in range(8)])
    n = len(items)

    def src_ap(it):
        k, e, i = it
        if k == "wg":
            return c.dram["l0_wg"][e, i * 128:(i + 1) * 128, :]
        if k == "wu":
            return c.dram["l0_wu"][e, i * 128:(i + 1) * 128, :]
        return c.dram["l0_wd"][e, i * 512:(i + 1) * 512, :].rearrange("(s p) w -> p s w", p=128)

    def dst_ap(it):
        k, e, i = it
        if k == "wg":
            return c.wg_bf[e, :, :, i * 128:(i + 1) * 128].rearrange("f p j -> p f j")
        if k == "wu":
            return c.wu_bf[e, :, :, i * 128:(i + 1) * 128].rearrange("f p j -> p f j")
        return c.wd_bf[e, i * 512:(i + 1) * 512, :].rearrange("(s p) w -> p s w", p=128)

    with ExitStack() as ps:
        fbuf = [ps.enter_context(nc.sbuf_tensor(f"ppf{i}_{c.uid}", [128, 4096], F32)) for i in range(2)]
        cbuf = [ps.enter_context(nc.sbuf_tensor(f"ppc{i}_{c.uid}", [128, 4096], BF16)) for i in range(2)]
        ph = new_phase(c)
        for i, it in enumerate(items):
            w = [f"ppv_{i-2}"] if i >= 2 else []
            ph.dma("sync", lambda e, i=i, it=it: e.dma_start(out=fbuf[i % 2][:], in_=src_ap(it)),
                   waits=w, ev=f"ppl_{i}")
            w = [f"ppl_{i}"] + ([f"pps_{i-2}"] if i >= 2 else [])
            if it[0] == "wd":
                ph.emit("vector", lambda e, i=i: e.tensor_scalar_mul(cbuf[i % 2][:], fbuf[i % 2][:], 0.5),
                        waits=w, ev=f"ppv_{i}")
            else:
                ph.emit("vector", lambda e, i=i: e.tensor_copy(cbuf[i % 2][:], fbuf[i % 2][:]),
                        waits=w, ev=f"ppv_{i}")
            inner = 128 if it[0] != "wd" else 1024
            ph.dma("gpsimd", lambda e, i=i, it=it, inner=inner: e.dma_start(
                out=dst_ap(it), in_=cbuf[i % 2].rearrange("p (a b) -> p a b", b=inner)),
                waits=[f"ppv_{i}"], ev=f"pps_{i}")
        ph.emit("gpsimd", None, waits=[f"pps_{n-1}"])
        ph.seal()
        ph.build()


def ph_pre_small(c, which):
    nc = c.nc
    items = []
    if which == "l0attn":
        for dc in range(8):
            items.append((c.dram["l0_woT"][dc * 128:(dc + 1) * 128, :],
                          c.l0_woT_bf[dc * 128:(dc + 1) * 128, :], 128, 1024))
        items.append((c.dram["l0_wq"][:], c.l0wq[:], 64, 64))
        items.append((c.dram["l0_wk"][:], c.l0wk[:], 64, 64))
        items.append((c.dram["l0_wv"][:], c.l0wv[:], 64, 64))
    else:
        for nm_src, nm_dst in [("l1_wqT", "l1_wqT_bf"), ("l1_wkT", "l1_wkT_bf"),
                               ("l1_wvT", "l1_wvT_bf"), ("l1_woT", "l1_woT_bf")]:
            for dc in range(8):
                items.append((c.dram[nm_src][dc * 128:(dc + 1) * 128, :],
                              getattr(c, nm_dst)[dc * 128:(dc + 1) * 128, :], 128, 1024))
        for h in range(NH):
            items.append((c.dram["l1_ki"][h], c.ki_bf[h], 64, 63))
            items.append((c.dram["l1_vi"][h], c.vi_bf[h], 63, 64))
    n = len(items)
    with ExitStack() as ps:
        fbuf = [ps.enter_context(nc.sbuf_tensor(f"psf{i}_{c.uid}", [128, 1024], F32)) for i in range(2)]
        cbuf = [ps.enter_context(nc.sbuf_tensor(f"psc{i}_{c.uid}", [128, 1024], BF16)) for i in range(2)]
        ph = new_phase(c)
        for i, (src, dst, pp, ff) in enumerate(items):
            w = [f"qsv_{i-2}"] if i >= 2 else []
            ph.dma("sync", lambda e, i=i, src=src, pp=pp, ff=ff: e.dma_start(
                out=fbuf[i % 2][0:pp, 0:ff], in_=src), waits=w, ev=f"qsl_{i}")
            w = [f"qsl_{i}"] + ([f"qss_{i-2}"] if i >= 2 else [])
            ph.emit("vector", lambda e, i=i, pp=pp, ff=ff: e.tensor_copy(
                cbuf[i % 2][0:pp, 0:ff], fbuf[i % 2][0:pp, 0:ff]), waits=w, ev=f"qsv_{i}")
            ph.dma("gpsimd", lambda e, i=i, dst=dst, pp=pp, ff=ff: e.dma_start(
                out=dst, in_=cbuf[i % 2][0:pp, 0:ff]), waits=[f"qsv_{i}"], ev=f"qss_{i}")
        ph.emit("gpsimd", None, waits=[f"qss_{n-1}"])
        ph.seal()
        ph.build()


def ph_pre_l1ffn(c):
    nc = c.nc
    items = [("w1", i) for i in range(8)] + [("w2", i) for i in range(32)]

    def w_src(it):
        k, i = it
        return (c.dram["l1_w1T"] if k == "w1" else c.dram["l1_w2T"])[i * 128:(i + 1) * 128, :]

    def w_dst(it):
        k, i = it
        if k == "w1":
            return c.w1s[:, :, i * 128:(i + 1) * 128].rearrange("h p j -> p h j")
        return c.w2s[:, :, i * 128:(i + 1) * 128].rearrange("w p j -> p w j")

    def w_ff(it):
        return 4096 if it[0] == "w1" else 1024

    n = len(items)
    with ExitStack() as ps:
        fbuf = [ps.enter_context(nc.sbuf_tensor(f"pqf{i}_{c.uid}", [128, 4096], F32)) for i in range(2)]
        abuf = [ps.enter_context(nc.sbuf_tensor(f"pqa{i}_{c.uid}", [128, 4096], BF16)) for i in range(2)]
        gbuf = [ps.enter_context(nc.sbuf_tensor(f"pqg0_{c.uid}", [128, 4096], BF16))] * 2
        arow = ps.enter_context(nc.sbuf_tensor(f"pqrow_{c.uid}", [1, 4096], F32))
        pall = ps.enter_context(nc.psum_tensor(f"pq_ps_{c.uid}", [128, 4096], F32))
        ph = new_phase(c)

        def emit_item(i, it):
            ff = w_ff(it)
            k, ic = it
            w = [f"qfa_{i-2}", f"qfg_{i-2}"] if i >= 2 else []
            ph.dma("sync", lambda e, i=i, it=it, ff=ff: e.dma_start(
                out=fbuf[i % 2][:, 0:ff], in_=w_src(it)), waits=w, ev=f"qfl_{i}")
            w = [f"qfl_{i}"] + ([f"qfm_{i-2}"] if i >= 2 else [])
            ph.emit("scalar", lambda e, i=i, ff=ff: e.activation(
                abuf[i % 2][:, 0:ff], fbuf[i % 2][:, 0:ff], AF.Abs),
                waits=w, ev=f"qfa_{i}")
            w = [f"qfl_{i}"] + ([f"qfs_{i-1}"] if i >= 1 else [])
            ph.emit("scalar", lambda e, i=i, ff=ff: e.activation(
                gbuf[i % 2][:, 0:ff], fbuf[i % 2][:, 0:ff], AF.Sign),
                waits=w, ev=f"qfg_{i}")
            ncol = 8 if k == "w1" else 2
            for cc in range(ncol):
                last = (cc == ncol - 1)
                w2w = [f"qfa_{i}"]
                if k == "w2" and ic == 0 and cc == 0:
                    w2w.append("qfe1")
                ph.emit("tensor", lambda e, i=i, cc=cc, k=k, ic=ic: e.matmul(
                    pall[:, cc * 512:(cc + 1) * 512] if k == "w1"
                    else pall[:, 2048 + cc * 512: 2048 + (cc + 1) * 512],
                    c.ones[:], abuf[i % 2][:, cc * 512:(cc + 1) * 512],
                    start=(ic == 0), stop=(ic == (7 if k == "w1" else 31))),
                    waits=w2w, ev=(f"qfm_{i}" if last else None))
            ph.dma("gpsimd", lambda e, i=i, it=it, ff=ff: e.dma_start(
                out=w_dst(it), in_=gbuf[i % 2][:, 0:ff].rearrange("p (a j) -> p a j", j=128)),
                waits=[f"qfg_{i}"], ev=f"qfs_{i}")

        for i in range(8):
            emit_item(i, items[i])
        for cc in range(8):
            ph.emit("vector", lambda e, cc=cc: e.tensor_scalar_mul(
                arow[:, cc * 512:(cc + 1) * 512], pall[0:1, cc * 512:(cc + 1) * 512], 1.0 / 1024),
                waits=[f"qfm_{7}"], ev=(f"qfe1" if cc == 7 else None))
        ph.dma("gpsimd", lambda e: e.dma_start(
            out=c.alpha1d.rearrange("a p x -> x (a p)"), in_=arow[:]),
            waits=["qfe1"], ev="qfe1d")
        for i in range(8, n):
            emit_item(i, items[i])
        for cc in range(2):
            ph.emit("vector", lambda e, cc=cc: e.tensor_scalar_mul(
                arow[:, cc * 512:(cc + 1) * 512], pall[0:1, 2048 + cc * 512: 2048 + (cc + 1) * 512],
                0.5 / 4096), waits=[f"qfm_{n-1}", "qfe1d"], ev=(f"qfe2" if cc == 1 else None))
        ph.dma("gpsimd", lambda e: e.dma_start(
            out=c.alpha2d.rearrange("a p x -> x (a p)"), in_=arow[0:1, 0:1024]),
            waits=["qfe2"], ev="qfe2d")
        k = 0
        for hc in range(32):
            ph.dma("sync", lambda e, hc=hc: e.dma_start(out=c.a1[:, hc:hc + 1], in_=c.alpha1d[hc]),
                   waits=["qfe1d"], ev=f"qfr_{k}"); k += 1
            ph.dma("sync", lambda e, hc=hc: e.dma_start(out=c.b1[:, hc:hc + 1], in_=c.dram["l1_b1"][hc]),
                   ev=f"qfr_{k}"); k += 1
        for cc in range(8):
            ph.dma("sync", lambda e, cc=cc: e.dma_start(out=c.a2h[:, cc:cc + 1], in_=c.alpha2d[cc]),
                   waits=["qfe2d"], ev=f"qfr_{k}"); k += 1
            ph.dma("sync", lambda e, cc=cc: e.dma_start(out=c.b2h[:, cc:cc + 1], in_=c.dram["l1_b2"][cc]),
                   ev=f"qfr_{k}"); k += 1
        ph.emit("vector", lambda e: e.tensor_scalar_mul(c.b2h[:], c.b2h[:], 0.5),
                waits=[f"qfr_{k-1}"], ev="qfb2")
        ph.emit("gpsimd", None, waits=["qfb2", f"qfs_{n-1}"])
        ph.seal()
        ph.build()


def ph_ln(c, tag):
    nc = c.nc
    with ExitStack() as ps:
        xbf = [ps.enter_context(nc.sbuf_tensor(f"lnx{i}_{c.uid}", [128, 512], BF16)) for i in range(2)]
        sq = [ps.enter_context(nc.sbuf_tensor(f"lnq{i}_{c.uid}", [128, 512], BF16)) for i in range(2)]
        msb = [ps.enter_context(nc.sbuf_tensor(f"lnm{i}_{c.uid}", [128, 512], F32)) for i in range(2)]
        msq = ps.enter_context(nc.sbuf_tensor(f"lnmsq_{c.uid}", [128, 512], F32))
        var = [ps.enter_context(nc.sbuf_tensor(f"lnv{i}_{c.uid}", [128, 512], F32)) for i in range(2)]
        rst = [ps.enter_context(nc.sbuf_tensor(f"lnr{i}_{c.uid}", [128, 512], F32)) for i in range(2)]
        t1 = [ps.enter_context(nc.sbuf_tensor(f"lnt{i}_{c.uid}", [128, 512], F32)) for i in range(2)]
        yb = [ps.enter_context(nc.sbuf_tensor(f"lny{i}_{c.uid}", [128, 4096], BF16)) for i in range(2)]
        ps_m = [ps.enter_context(nc.psum_tensor(f"lnpm{i}_{c.uid}", [128, 512], F32)) for i in range(2)]
        ps_s = [ps.enter_context(nc.psum_tensor(f"lnps{i}_{c.uid}", [128, 512], F32)) for i in range(2)]

        def xsl(dc, tb):
            return c.xt[:, dc * T + tb * 512: dc * T + tb * 512 + 512]

        ph = new_phase(c)
        g = tag
        for tb in range(8):
            for dc in range(8):
                i = tb * 8 + dc
                w = [f"{g}mm_{i-2}", f"{g}sq_{i-2}"] if i >= 2 else []
                ph.emit("scalar", lambda e, dc=dc, tb=tb, i=i: e.copy(xbf[i % 2][:], xsl(dc, tb)),
                        waits=w, ev=f"{g}cp_{i}")
                w = [f"{g}cp_{i}"] + ([f"{g}sm_{i-2}"] if i >= 2 else [])
                ph.emit("vector", lambda e, i=i: e.tensor_mul(sq[i % 2][:], xbf[i % 2][:], xbf[i % 2][:]),
                        waits=w, ev=f"{g}sq_{i}")
                w = [f"{g}cp_{i}"] + ([f"{g}st1_{tb-2}"] if tb >= 2 else [])
                ph.emit("tensor", lambda e, i=i, dc=dc, tb=tb: e.matmul(
                    ps_m[tb % 2][:], c.ones[:], xbf[i % 2][:], start=(dc == 0), stop=(dc == 7)),
                    waits=w, ev=f"{g}mm_{i}")
                w = [f"{g}sq_{i}"] + ([f"{g}st2_{tb-2}"] if tb >= 2 else [])
                ph.emit("tensor", lambda e, i=i, dc=dc, tb=tb: e.matmul(
                    ps_s[tb % 2][:], c.ones[:], sq[i % 2][:], start=(dc == 0), stop=(dc == 7)),
                    waits=w, ev=f"{g}sm_{i}")
            ph.emit("vector", lambda e, tb=tb: e.tensor_scalar_mul(msb[tb % 2][:], ps_m[tb % 2][:], 1.0 / 1024),
                    waits=[f"{g}mm_{8*tb+7}"], ev=f"{g}st1_{tb}")
            ph.emit("vector", lambda e, tb=tb: e.tensor_scalar_mul(var[tb % 2][:], ps_s[tb % 2][:], 1.0 / 1024),
                    waits=[f"{g}sm_{8*tb+7}"], ev=f"{g}st2_{tb}")
            ph.emit("vector", lambda e, tb=tb: e.tensor_mul(msq[:], msb[tb % 2][:], msb[tb % 2][:]),
                    waits=[f"{g}st1_{tb}", f"{g}st2_{tb}"], ev=f"{g}m2_{tb}")
            ph.emit("vector", lambda e, tb=tb: e.tensor_sub(var[tb % 2][:], var[tb % 2][:], msq[:]),
                    waits=[f"{g}m2_{tb}"], ev=f"{g}zz_{tb}")
            ph.emit("scalar", lambda e, tb=tb: e.activation(var[tb % 2][:], var[tb % 2][:], AF.Ln,
                                                            bias=c.eps[:]),
                    waits=[f"{g}zz_{tb}"], ev=f"{g}ln_{tb}")
            ph.emit("scalar", lambda e, tb=tb: e.activation(rst[tb % 2][:], var[tb % 2][:], AF.Exp,
                                                            scale=-0.5),
                    waits=[f"{g}ln_{tb}"], ev=f"{g}rs_{tb}")
            for dc in range(8):
                w = [f"{g}st1_{tb}"] + ([f"{g}yw_{tb-2}"] if tb >= 2 else [])
                if dc >= 2:
                    w.append(f"{g}ya_{tb}_{dc-2}")
                elif tb > 0:
                    w.append(f"{g}ya_{tb-1}_{dc+6}")
                ph.emit("vector", lambda e, dc=dc, tb=tb: e.tensor_sub(t1[dc % 2][:], xsl(dc, tb), msb[tb % 2][:]),
                        waits=w, ev=f"{g}t1_{tb}_{dc}")
                ph.emit("vector", lambda e, dc=dc, tb=tb: e.tensor_mul(
                    yb[tb % 2][:, dc * 512:(dc + 1) * 512], t1[dc % 2][:], rst[tb % 2][:]),
                    waits=[f"{g}t1_{tb}_{dc}", f"{g}rs_{tb}"], ev=f"{g}ya_{tb}_{dc}")
            ph.dma("gpsimd", lambda e, tb=tb: e.dma_start(
                out=c.yT[:, tb * 512:(tb + 1) * 512].rearrange("(dc p) t -> p dc t", p=128),
                in_=yb[tb % 2].rearrange("p (dc t) -> p dc t", t=512)),
                waits=[f"{g}ya_{tb}_{7}"], ev=f"{g}yw_{tb}")
        ph.emit("gpsimd", None, waits=[f"{g}yw_{7}"])
        ph.seal()
        ph.build()


def ph_l0ffn(c, tag):
    nc = c.nc
    g = tag
    with ExitStack() as ps:
        wgb = [ps.enter_context(nc.sbuf_tensor(f"wgb{i}_{c.uid}", [128, 1024], BF16)) for i in range(4)]
        wub = [ps.enter_context(nc.sbuf_tensor(f"wub{i}_{c.uid}", [128, 1024], BF16)) for i in range(4)]
        wdb = [ps.enter_context(nc.sbuf_tensor(f"wdb{i}_{c.uid}", [128, 1024], BF16)) for i in range(4)]
        ybuf = ps.enter_context(nc.sbuf_tensor(f"ybuf_{c.uid}", [128, 4096], BF16))
        slg = [ps.enter_context(nc.sbuf_tensor(f"slg{i}_{c.uid}", [128, 512], BF16)) for i in range(2)]
        hbuf = ps.enter_context(nc.sbuf_tensor(f"hbuf_{c.uid}", [128, 32 * 512], BF16))
        pall = ps.enter_context(nc.psum_tensor(f"ff_ps_{c.uid}", [128, 4096], F32))

        def bank(b):
            return pall[:, b * 512:(b + 1) * 512]

        ph = new_phase(c)
        for it in range(64):
            tb, e = it // 8, it % 8
            if e == 0:
                w = [f"{g}u_{32*it-1}"] if it > 0 else []
                ph.dma("gpsimd", lambda e_, tb=tb: e_.dma_start(
                    out=ybuf.rearrange("p (dc t) -> p dc t", t=512),
                    in_=c.yT[:, tb * 512:(tb + 1) * 512].rearrange("(dc p) t -> p dc t", p=128)),
                    waits=w, ev=f"{g}y_{tb}")
            for f in range(32):
                idx = it * 32 + f
                w = [f"{g}g_{idx-4}"] if idx >= 4 else []
                ph.dma("sync", lambda s, e=e, f=f: s.dma_start(out=wgb[f % 4][:], in_=c.wg_bf[e, f]),
                       waits=w, ev=f"{g}lg_{idx}")
                w = [f"{g}u_{idx-4}"] if idx >= 4 else []
                ph.dma("sync", lambda s, e=e, f=f: s.dma_start(out=wub[f % 4][:], in_=c.wu_bf[e, f]),
                       waits=w, ev=f"{g}lu_{idx}")
                if idx >= 4:
                    pidx = idx - 4
                    if pidx % 32 == 31:
                        w = [f"{g}bD_{pidx // 32}_7"]
                    else:
                        w = [f"{g}b_{pidx}"]
                else:
                    w = []
                ph.dma("gpsimd", lambda s, e=e, f=f: s.dma_start(
                    out=wdb[f % 4][:], in_=c.wd_bf[e, f * 128:(f + 1) * 128, :]),
                    waits=w, ev=f"{g}ld_{idx}")
                w = [f"{g}lg_{idx}", f"{g}y_{tb}"]
                if idx >= 2:
                    w.append(f"{g}si_{idx-2}")
                if f == 0 and it > 0:
                    w.append(f"{g}df_{it-1}_3")
                for dc in range(8):
                    ph.emit("tensor", lambda t, f=f, dc=dc: t.matmul(
                        bank(f % 2), wgb[f % 4][:, dc * 128:(dc + 1) * 128],
                        ybuf[:, dc * 512:(dc + 1) * 512], start=(dc == 0), stop=(dc == 7)),
                        waits=(w if dc == 0 else []), ev=(f"{g}g_{idx}" if dc == 7 else None))
                w = [f"{g}lu_{idx}"]
                if idx >= 2:
                    w.append(f"{g}h_{idx-2}")
                if f == 0 and it > 0:
                    w.append(f"{g}df_{it-1}_3")
                for dc in range(8):
                    ph.emit("tensor", lambda t, f=f, dc=dc: t.matmul(
                        bank(2 + f % 2), wub[f % 4][:, dc * 128:(dc + 1) * 128],
                        ybuf[:, dc * 512:(dc + 1) * 512], start=(dc == 0), stop=(dc == 7)),
                        waits=(w if dc == 0 else []), ev=(f"{g}u_{idx}" if dc == 7 else None))
                w = [f"{g}g_{idx}"]
                if idx >= 2:
                    w.append(f"{g}h_{idx-2}")
                ph.emit("scalar", lambda s, f=f: s.activation(slg[f % 2][:], bank(f % 2), AF.Silu),
                        waits=w, ev=f"{g}si_{idx}")
                w = [f"{g}u_{idx}", f"{g}si_{idx}"]
                if f == 0 and it > 0:
                    w.append(f"{g}bD_{it-1}_7")
                ph.emit("vector", lambda v, f=f: v.tensor_mul(
                    hbuf[:, f * 512:(f + 1) * 512], slg[f % 2][:], bank(2 + f % 2)),
                    waits=w, ev=f"{g}h_{idx}")
            for f in range(32):
                idx = it * 32 + f
                order = [4, 5, 6, 7, 0, 1, 2, 3] if f == 0 else list(range(8))
                for oi, dc in enumerate(order):
                    w = []
                    if oi == 0:
                        w = [f"{g}ld_{idx}", f"{g}h_{idx}"]
                    if f == 0 and dc == 4 and it > 0:
                        w.append(f"{g}df_{it-1}_7")
                    if f == 0 and dc == 0:
                        w.append(f"{g}si_{32*it+31}")
                        w.append(f"{g}h_{32*it+31}")
                    last_of_f = (oi == len(order) - 1)
                    ph.emit("tensor", lambda t, f=f, dc=dc: t.matmul(
                        bank(dc), wdb[f % 4][:, dc * 128:(dc + 1) * 128],
                        hbuf[:, f * 512:(f + 1) * 512], start=(f == 0), stop=(f == 31)),
                        waits=w,
                        ev=(f"{g}b_{idx}" if (f < 31 and last_of_f) else
                            (f"{g}bD_{it}_{dc}" if f == 31 else None)))
            for dc in range(8):
                ph.emit("vector", lambda v, dc=dc, tb=tb: v.scalar_tensor_tensor(
                    c.xt[:, dc * T + tb * 512: dc * T + tb * 512 + 512],
                    bank(dc), 1.0,
                    c.xt[:, dc * T + tb * 512: dc * T + tb * 512 + 512],
                    ALU.mult, ALU.add),
                    waits=[f"{g}bD_{it}_{dc}"], ev=f"{g}df_{it}_{dc}")
        ph.seal()
        ph.build()


def ph_l1ffn(c, tag):
    nc = c.nc
    g = tag
    with ExitStack() as ps:
        w1b = [ps.enter_context(nc.sbuf_tensor(f"w1b{i}_{c.uid}", [128, 1024], BF16)) for i in range(4)]
        w2b = [ps.enter_context(nc.sbuf_tensor(f"w2b{i}_{c.uid}", [128, 4096], BF16)) for i in range(2)]
        ybuf = [ps.enter_context(nc.sbuf_tensor(f"yfb0_{c.uid}", [128, 4096], BF16))] * 2
        h1 = ps.enter_context(nc.sbuf_tensor(f"h1buf_{c.uid}", [128, 32 * 512], BF16))
        tmp = [ps.enter_context(nc.sbuf_tensor(f"oftmp{i}_{c.uid}", [128, 512], F32)) for i in range(2)]
        pall = ps.enter_context(nc.psum_tensor(f"l1f_ps_{c.uid}", [128, 4096], F32))

        def bank(b):
            return pall[:, b * 512:(b + 1) * 512]

        ph = new_phase(c)
        for tb in range(8):
            w = [f"{g}hm_{32*(tb-1)+31}"] if tb >= 1 else []
            ph.dma("gpsimd", lambda e, tb=tb: e.dma_start(
                out=ybuf[tb % 2].rearrange("p (dc t) -> p dc t", t=512),
                in_=c.yT[:, tb * 512:(tb + 1) * 512].rearrange("(dc p) t -> p dc t", p=128)),
                waits=w, ev=f"{g}y_{tb}")
            for hc in range(32):
                i = tb * 32 + hc
                w = [f"{g}hm_{i-4}"] if i >= 4 else []
                ph.dma("sync", lambda s, hc=hc: s.dma_start(out=w1b[hc % 4][:], in_=c.w1s[hc]),
                       waits=w, ev=f"{g}l1_{i}")
                w = [f"{g}l1_{i}", f"{g}y_{tb}"]
                if i >= 2:
                    w.append(f"{g}he_{i-2}")
                for dc in range(8):
                    ph.emit("tensor", lambda t, hc=hc, dc=dc, tb=tb: t.matmul(
                        bank(hc % 2), w1b[hc % 4][:, dc * 128:(dc + 1) * 128],
                        ybuf[tb % 2][:, dc * 512:(dc + 1) * 512], start=(dc == 0), stop=(dc == 7)),
                        waits=(w if dc == 0 else []), ev=(f"{g}hm_{i}" if dc == 7 else None))
                w = [f"{g}hm_{i}"]
                if tb > 0 and hc == 0:
                    w.append(f"{g}om_{8*tb-1}")
                ph.emit("scalar", lambda s, hc=hc: s.activation(
                    h1[:, hc * 512:(hc + 1) * 512], bank(hc % 2), AF.Gelu,
                    scale=c.a1[:, hc:hc + 1], bias=c.b1[:, hc:hc + 1]),
                    waits=w, ev=f"{g}he_{i}")
            for cc in range(8):
                j = tb * 8 + cc
                w = [f"{g}om_{j-2}"] if j >= 2 else []
                ph.dma("sync", lambda s, cc=cc: s.dma_start(out=w2b[cc % 2][:], in_=c.w2s[cc]),
                       waits=w, ev=f"{g}l2_{j}")
                w = [f"{g}l2_{j}"]
                if j >= 2:
                    w.append(f"{g}oe_{j-2}")
                for hc in range(32):
                    wh = (w if hc == 0 else []) + [f"{g}he_{32*tb+hc}"]
                    ph.emit("tensor", lambda t, cc=cc, hc=hc: t.matmul(
                        bank(2 + cc % 2), w2b[cc % 2][:, hc * 128:(hc + 1) * 128],
                        h1[:, hc * 512:(hc + 1) * 512], start=(hc == 0), stop=(hc == 31)),
                        waits=wh, ev=(f"{g}om_{j}" if hc == 31 else None))
                w = [f"{g}om_{j}"]
                if j >= 2:
                    w.append(f"{g}ox_{j-2}")
                ph.emit("scalar", lambda s, cc=cc: s.activation(
                    tmp[cc % 2][:], bank(2 + cc % 2), AF.Identity,
                    scale=c.a2h[:, cc:cc + 1], bias=c.b2h[:, cc:cc + 1]),
                    waits=w, ev=f"{g}oe_{j}")
                ph.emit("vector", lambda v, cc=cc, tb=tb: v.tensor_add(
                    c.xt[:, cc * T + tb * 512: cc * T + tb * 512 + 512],
                    c.xt[:, cc * T + tb * 512: cc * T + tb * 512 + 512],
                    tmp[cc % 2][:]),
                    waits=[f"{g}oe_{j}"], ev=f"{g}ox_{j}")
        ph.seal()
        ph.build()


def ph_attn(c, layer, tag):
    nc = c.nc
    g = tag

    def score_tiles(qb):
        out = []
        if layer == 0:
            for kt in range(0, 2 * qb + 2):
                mv = 0 if kt == 2 * qb else (1 if kt == 2 * qb + 1 else None)
                out.append((kt, mv, False))
        else:
            if qb == 0:
                out.append((0, 3, True))
            for kt in (2 * qb - 1, 2 * qb, 2 * qb + 1):
                if kt >= 0:
                    out.append((kt, kt - 2 * qb + 1, False))
        return out

    def av_tiles(qb, qh):
        out = []
        if layer == 0:
            out = [(kt, False) for kt in range(0, 2 * qb + qh + 1)]
        else:
            if qb == 0 and qh == 0:
                out.append((0, True))
            for kt in (2 * qb + qh - 1, 2 * qb + qh):
                if kt >= 0:
                    out.append((kt, False))
        return out

    def slot_of(qb, kt, isinit):
        if layer == 0:
            return kt
        if isinit:
            return 3
        return kt - (2 * qb - 1) if qb > 0 else kt + 1

    SCT = [score_tiles(qb) for qb in range(16)]
    AVT = [[av_tiles(qb, qh) for qh in range(2)] for qb in range(16)]
    nsl = 32 if layer == 0 else 4

    batches = []
    sidx = {}
    si = 0
    tile_batch = {}
    for qb in range(16):
        run = []
        for (kt, mv, isinit) in SCT[qb]:
            sidx[(qb, kt, isinit)] = si
            ent = (si, kt, mv, isinit, slot_of(qb, kt, isinit))
            si += 1
            if isinit:
                batches.append((qb, [ent]))
            else:
                if run and (len(run) == 4 or run[-1][4] + 1 != ent[4]):
                    batches.append((qb, run))
                    run = []
                run.append(ent)
        if run:
            batches.append((qb, run))
    NB = len(batches)
    for bi, (qb, ents) in enumerate(batches):
        for ent in ents:
            tile_batch[ent[0]] = bi
    ns_sc = si

    with ExitStack() as ps:
        if layer == 0:
            yh = ps.enter_context(nc.sbuf_tensor(f"yh_{c.uid}", [64, 4096], BF16))
        else:
            ybf = [ps.enter_context(nc.sbuf_tensor(f"atyb{i}_{c.uid}", [128, 4096], BF16)) for i in range(2)]
            wqb = ps.enter_context(nc.sbuf_tensor(f"wqb_{c.uid}", [128, 512], BF16))
            wkb = ps.enter_context(nc.sbuf_tensor(f"wkb_{c.uid}", [128, 512], BF16))
            wvb = ps.enter_context(nc.sbuf_tensor(f"wvb_{c.uid}", [128, 512], BF16))
            kib = ps.enter_context(nc.sbuf_tensor(f"kib_{c.uid}", [65, 63], BF16))
            vib = ps.enter_context(nc.sbuf_tensor(f"vib_{c.uid}", [63, 65], BF16))
            ksqi = ps.enter_context(nc.sbuf_tensor(f"ksqi_{c.uid}", [64, 63], BF16))
        qa = ps.enter_context(nc.sbuf_tensor(f"qa_{c.uid}", [65, 4096], BF16))
        ka = ps.enter_context(nc.sbuf_tensor(f"ka_{c.uid}", [65, 4096], BF16))
        va = [ps.enter_context(nc.sbuf_tensor(f"va{i}_{c.uid}", [64, 512], BF16)) for i in range(2)]
        vtm = ps.enter_context(nc.sbuf_tensor(f"vtm_{c.uid}", [128, 32 * 65], BF16))
        abuf = ps.enter_context(nc.sbuf_tensor(f"abuf_{c.uid}", [128, nsl * 256], BF16))
        o_h = [ps.enter_context(nc.sbuf_tensor(f"oh{i}_{c.uid}", [64, 4096], BF16)) for i in range(2)]
        wo_b = [ps.enter_context(nc.sbuf_tensor(f"wob{i}_{c.uid}", [64, 1024], BF16)) for i in range(2)]
        otm = [ps.enter_context(nc.sbuf_tensor(f"otm{i}_{c.uid}", [128, 64], BF16)) for i in range(2)]
        ksq = [ps.enter_context(nc.sbuf_tensor(f"ksq{i}_{c.uid}", [64, 512], BF16)) for i in range(2)]
        kmax8 = ps.enter_context(nc.sbuf_tensor(f"kmax8_{c.uid}", [128, 12], F32))
        kmaxv = ps.enter_context(nc.sbuf_tensor(f"kmaxv_{c.uid}", [128, 1], F32))
        qmaxv = ps.enter_context(nc.sbuf_tensor(f"qmaxv_{c.uid}", [128, 1], F32))
        cb2 = ps.enter_context(nc.sbuf_tensor(f"cb2_{c.uid}", [128, 1], F32))
        cb2l = ps.enter_context(nc.sbuf_tensor(f"cb2l_{c.uid}", [128, 1], F32))
        cval = ps.enter_context(nc.sbuf_tensor(f"cval_{c.uid}", [128, 1], F32))
        rcp = [ps.enter_context(nc.sbuf_tensor(f"rcp{i}_{c.uid}", [128, 1], F32)) for i in range(2)]
        wtmp = [ps.enter_context(nc.sbuf_tensor(f"wtmp{i}_{c.uid}", [128, 512], F32)) for i in range(2)]
        pf32 = ps.enter_context(nc.psum_tensor(f"at_ps_{c.uid}", [128, 3072], F32))
        pbf = ps.enter_context(nc.psum_tensor(f"at_pb_{c.uid}", [128, 2048], BF16))

        def fb(b):
            return pf32[:, b * 512:(b + 1) * 512]

        def reg(r):
            return pf32[:, r * 1024:(r + 1) * 1024]

        def bb(b):
            return pbf[:, b * 1024:(b + 1) * 1024]

        woT = c.l0_woT_bf if layer == 0 else c.l1_woT_bf
        ph = new_phase(c)

        ph.emit("gpsimd", lambda e: e.memset(ka[64:65, :], -1.0), inc=0)
        ph.emit("gpsimd", lambda e: e.memset(qa[64:65, :], 0.0), inc=0)
        for kc in range(32):
            ph.emit("gpsimd", lambda e, kc=kc: e.memset(vtm[:, kc * 65 + 64: kc * 65 + 65], 1.0),
                    inc=0)
        if layer == 1:
            ph.emit("gpsimd", lambda e: e.memset(kib[64:65, :], -1.0), inc=0)
            ph.emit("gpsimd", lambda e: e.memset(vib[:, 64:65], 1.0), inc=0)

        for h in range(NH):
            hh = h % 2
            p = h // 2
            exlast_prev = f"{g}ex_{h-1}_{NB-1}"
            w = [f"{g}womm_{p-1}_63"] if p > 0 else []
            ph.dma("sync", lambda e, h=h, hh=hh: e.dma_start(
                out=wo_b[hh][:], in_=woT[64 * h:64 * h + 64, :]), waits=w, ev=f"{g}wol_{h}")
            if layer == 0:
                w = [f"{g}pv_{h-1}_7"] if h > 0 else []
                ph.dma("gpsimd", lambda e, h=h: e.dma_start(out=yh[:], in_=c.yT[64 * h:64 * h + 64, :]),
                       waits=w, ev=f"{g}yh_{h}")
            else:
                for wsrc, wdst, wn in ((c.l1_wqT_bf, wqb, "q"), (c.l1_wkT_bf, wkb, "k"),
                                       (c.l1_wvT_bf, wvb, "v")):
                    w = [f"{g}pv_{h-1}_7"] if h > 0 else []
                    ph.dma("sync", lambda e, wsrc=wsrc, wdst=wdst, h=h: e.dma_start(
                        out=wdst.rearrange("p (dc j) -> p dc j", j=64),
                        in_=wsrc[:, 64 * h:64 * h + 64].rearrange("(dc p) j -> p dc j", p=128)),
                        waits=w, ev=f"{g}wl{wn}_{h}")
                w = [f"{g}ksqi_{h-1}", f"{g}sm_{h-1}_0"] if h > 0 else []
                ph.dma("gpsimd", lambda e, h=h: e.dma_start(out=kib[0:64, :], in_=c.ki_bf[h]),
                       waits=w, ev=f"{g}kil_{h}")
                w = [f"{g}avg_{h-1}_0"] if h > 0 else []
                ph.dma("gpsimd", lambda e, h=h: e.dma_start(out=vib[0:63, 0:64], in_=c.vi_bf[h]),
                       waits=w, ev=f"{g}vil_{h}")

            for tb in range(8):
                i = 8 * h + tb
                if layer == 1:
                    w = [f"{g}pv_{(i-2)//8}_{(i-2)%8}"] if i >= 2 else []
                    ph.dma("gpsimd", lambda e, tb=tb: e.dma_start(
                        out=ybf[tb % 2].rearrange("p (dc t) -> p dc t", t=512),
                        in_=c.yT[:, tb * 512:(tb + 1) * 512].rearrange("(dc p) t -> p dc t", p=128)),
                        waits=w, ev=f"{g}yl_{h}_{tb}")
                for (bnk, wn, wtile, pev, bev) in (
                        (0, "q", None, "pq", "eq"), (1, "k", None, "pk", "ek"), (2, "v", None, "pv", "ev")):
                    w = []
                    if layer == 0:
                        w.append(f"{g}yh_{h}")
                        lhs = {"q": c.l0wq, "k": c.l0wk, "v": c.l0wv}[wn]
                    else:
                        w.append(f"{g}yl_{h}_{tb}")
                        w.append(f"{g}wl{wn}_{h}")
                        lhs = {"q": wqb, "k": wkb, "v": wvb}[wn]
                    if tb > 0:
                        w.append(f"{g}{bev}_{h}_{tb-1}")
                    elif h > 0:
                        if hh == 0:
                            w.append(f"{g}woev_{p-1}_62")
                            w.append(f"{g}wog_{p-1}_63")
                        else:
                            w.append(exlast_prev)
                    if layer == 0:
                        ph.emit("tensor", lambda t, tb=tb, bnk=bnk, lhs=lhs: t.matmul(
                            fb(bnk)[0:64, :], lhs[:], yh[:, tb * 512:(tb + 1) * 512],
                            start=True, stop=True), waits=w, ev=f"{g}{pev}_{h}_{tb}")
                    else:
                        for dc in range(8):
                            ph.emit("tensor", lambda t, tb=tb, dc=dc, bnk=bnk, lhs=lhs: t.matmul(
                                fb(bnk)[0:64, :], lhs[:, dc * 64:(dc + 1) * 64],
                                ybf[tb % 2][:, dc * 512:(dc + 1) * 512],
                                start=(dc == 0), stop=(dc == 7)),
                                waits=(w if dc == 0 else []),
                                ev=(f"{g}{pev}_{h}_{tb}" if dc == 7 else None))
                w = [f"{g}pq_{h}_{tb}"]
                if tb == 0 and h > 0:
                    w.append(f"{g}avg_{h-1}_31")
                ph.emit("scalar", lambda v, h=h, tb=tb: v.activation(
                    qa[0:64, tb * 512:(tb + 1) * 512], fb(0)[0:64, :], AF.Identity,
                    scale=0.125),
                    waits=w, ev=f"{g}eq_{h}_{tb}")
                w = [f"{g}pk_{h}_{tb}"]
                if tb == 0 and h > 0:
                    w.append(f"{g}avg_{h-1}_31")
                ph.emit("vector", lambda v, h=h, tb=tb: v.tensor_copy(
                    ka[0:64, tb * 512:(tb + 1) * 512], fb(1)[0:64, :]),
                    waits=w, ev=f"{g}ek_{h}_{tb}")
                w = [f"{g}pv_{h}_{tb}"]
                if tb >= 2:
                    w.append(f"{g}vt_{h}_{tb-2}_3")
                elif h > 0:
                    w.append(f"{g}vt_{h-1}_{tb+6}_3")
                ph.emit("scalar", lambda s, tb=tb: s.copy(va[tb % 2][0:64, :], fb(2)[0:64, :]),
                        waits=w, ev=f"{g}ev_{h}_{tb}")
                w = [f"{g}ek_{h}_{tb}"]
                if tb >= 2:
                    w.append(f"{g}kn_{h}_{tb-2}")
                elif h > 0:
                    w.append(f"{g}qn_{h-1}_{tb+6}")
                ph.emit("vector", lambda v, tb=tb: v.tensor_mul(
                    ksq[tb % 2][0:64, :], ka[0:64, tb * 512:(tb + 1) * 512],
                    ka[0:64, tb * 512:(tb + 1) * 512]),
                    waits=w, ev=f"{g}ksq_{h}_{tb}")
                w = [f"{g}ksq_{h}_{tb}"]
                if tb > 0:
                    w.append(f"{g}knr_{h}_{tb-1}")
                elif h > 0:
                    if hh == 0:
                        w.append(f"{g}wog_{p-1}_63")
                    else:
                        w.append(exlast_prev)
                ph.emit("tensor", lambda t, tb=tb: t.matmul(
                    fb(3), c.ones[0:64, :], ksq[tb % 2][0:64, :], start=True, stop=True),
                    waits=w, ev=f"{g}kn_{h}_{tb}")
                ph.emit("vector", lambda v, tb=tb: v.reduce_max(kmax8[:, tb:tb + 1], fb(3), axis=AX.X),
                        waits=[f"{g}kn_{h}_{tb}"], ev=f"{g}knr_{h}_{tb}")
                def emit_vtr(tb, h=h):
                    for j in range(4):
                        kc = 4 * tb + j
                        w = [f"{g}ev_{h}_{tb}"]
                        if kc >= 2:
                            w.append(f"{g}vau_{h}_{kc-2}")
                        elif h > 0:
                            w.append(f"{g}oev_{h-1}_31")
                        ph.emit("tensor", lambda t, tb=tb, j=j, kc=kc: t.transpose(
                            bb(kc % 2)[:, 0:64], va[tb % 2][0:64, j * 128:(j + 1) * 128],
                            c.ident[0:64, 0:64]),
                            waits=w, ev=f"{g}vt_{h}_{tb}_{j}")
                        w = [f"{g}vt_{h}_{tb}_{j}"]
                        if h > 0 and kc == 0:
                            w.append(f"{g}avg_{h-1}_31")
                        ph.emit("scalar", lambda v, kc=kc: v.copy(
                            vtm[:, kc * 65: kc * 65 + 64], bb(kc % 2)[:, 0:64]),
                            waits=w, ev=f"{g}vau_{h}_{kc}")
                if tb > 0:
                    emit_vtr(tb - 1)
                if tb == 7:
                    emit_vtr(7)
            if layer == 1:
                ph.emit("scalar", lambda s: s.activation(ksqi[0:64, :], kib[0:64, :], AF.Square),
                        waits=[f"{g}kil_{h}"], ev=f"{g}ksqi_{h}")
                ph.emit("tensor", lambda t: t.matmul(
                    fb(3)[:, 0:63], c.ones[0:64, :], ksqi[0:64, :], start=True, stop=True),
                    waits=[f"{g}ksqi_{h}", f"{g}knr_{h}_7"], ev=f"{g}kni_{h}")
                ph.emit("vector", lambda v: v.reduce_max(kmax8[:, 8:9], fb(3)[:, 0:63], axis=AX.X),
                        waits=[f"{g}kni_{h}"], ev=f"{g}knri_{h}")
            ncol = 8 if layer == 0 else 9
            w = [f"{g}knr_{h}_7"] + ([f"{g}knri_{h}"] if layer == 1 else [])
            ph.emit("vector", lambda v, ncol=ncol: v.reduce_max(kmaxv[:], kmax8[:, 0:ncol], axis=AX.X),
                    waits=w, ev=f"{g}kmx_{h}")

            for tb in range(8):
                w = [f"{g}eq_{h}_{tb}",
                     f"{g}kn_{h}_7" if layer == 0 else f"{g}kni_{h}"]
                if tb >= 2:
                    w.append(f"{g}qn_{h}_{tb-2}")
                ph.emit("vector", lambda v, tb=tb: v.tensor_mul(
                    ksq[tb % 2][0:64, :], qa[0:64, tb * 512:(tb + 1) * 512],
                    qa[0:64, tb * 512:(tb + 1) * 512]),
                    waits=w, ev=f"{g}qsq_{h}_{tb}")
                w = [f"{g}qsq_{h}_{tb}"]
                if tb == 0:
                    w.append(f"{g}eq_{h}_7")
                elif tb == 1:
                    w.append(f"{g}ek_{h}_7")
                elif tb == 2:
                    w.append(f"{g}ev_{h}_7")
                elif tb == 3:
                    w.append(f"{g}knri_{h}" if layer == 1 else f"{g}knr_{h}_7")
                else:
                    w.append(f"{g}qred_{h}_0")
                ph.emit("tensor", lambda t, tb=tb: t.matmul(
                    fb(tb % 4), c.ones[0:64, :], ksq[tb % 2][0:64, :], start=True, stop=True),
                    waits=w, ev=f"{g}qn_{h}_{tb}")
                if tb % 4 == 3:
                    ph.emit("vector", lambda v, tb=tb: v.reduce_max(
                        kmax8[:, 10 + tb // 4:11 + tb // 4], pf32[:, 0:2048], axis=AX.X),
                        waits=[f"{g}qn_{h}_{tb}"], ev=f"{g}qred_{h}_{tb//4}")
            ph.emit("vector", lambda v: v.reduce_max(qmaxv[:], kmax8[:, 10:12], axis=AX.X),
                    waits=[f"{g}qred_{h}_1"], ev=f"{g}qmx_{h}")
            ph.emit("vector", lambda v: v.tensor_mul(cb2[:], qmaxv[:], kmaxv[:]),
                    waits=[f"{g}qmx_{h}", f"{g}kmx_{h}"], ev=f"{g}cb2_{h}")
            ph.emit("scalar", lambda s: s.activation(cb2l[:], cb2[:], AF.Ln, bias=c.tiny[:]),
                    waits=[f"{g}cb2_{h}"], ev=f"{g}cln_{h}")
            ph.emit("scalar", lambda s: s.activation(cval[:], cb2l[:], AF.Exp, scale=0.5),
                    waits=[f"{g}cln_{h}"], ev=f"{g}cex_{h}")
            w = [f"{g}cex_{h}"]
            if h > 0:
                w.append(f"{g}avg_{h-1}_31")
            ph.emit("scalar", lambda s: s.activation(
                qa[64:65, :], qa[64:65, :], AF.Identity, scale=0.0, bias=cval[64:65, 0:1]),
                waits=w, ev=f"{g}c_{h}")

            for bi, (qb, ents) in enumerate(batches):
                r = bi % 2
                nb = len(ents)
                last_si = ents[-1][0]
                for z, (si_, kt, mv, isinit, slot) in enumerate(ents):
                    w = []
                    if z == 0:
                        w = [f"{g}c_{h}", f"{g}ek_{h}_7"]
                        if bi >= 2:
                            w.append(f"{g}ex_{h}_{bi-2}")
                        else:
                            w.append(f"{g}qred_{h}_1")
                    if isinit:
                        w = w + [f"{g}kil_{h}"]
                        ph.emit("tensor", lambda t, qb=qb, r=r, z=z: t.matmul(
                            reg(r)[0:63, z * 256:(z + 1) * 256], kib[:, 0:63],
                            qa[:, qb * 256:(qb + 1) * 256], start=True, stop=True),
                            waits=w, ev=f"{g}sm_{h}_{si_}")
                    else:
                        ph.emit("tensor", lambda t, qb=qb, kt=kt, r=r, z=z: t.matmul(
                            reg(r)[:, z * 256:(z + 1) * 256], ka[:, kt * 128:(kt + 1) * 128],
                            qa[:, qb * 256:(qb + 1) * 256], start=True, stop=True),
                            waits=w, ev=f"{g}sm_{h}_{si_}")
                nmk = sum(1 for e2 in ents if e2[2] is not None)
                if nmk:
                    assert ents[-1][2] is not None
                for z, (si_, kt, mv, isinit, slot) in enumerate(ents):
                    if mv is None:
                        continue
                    rows = 64 if isinit else 128
                    if isinit:
                        msk_ap = c.l1mi[:]
                    elif layer == 0:
                        msk_ap = c.l0m[:, mv * 256:(mv + 1) * 256]
                    else:
                        msk_ap = c.l1m[:, mv * 256:(mv + 1) * 256]
                    ph.emit("vector", lambda v, r=r, z=z, rows=rows, msk_ap=msk_ap: v.tensor_add(
                        reg(r)[0:rows, z * 256:(z + 1) * 256],
                        reg(r)[0:rows, z * 256:(z + 1) * 256], msk_ap),
                        waits=[f"{g}sm_{h}_{last_si}"],
                        ev=(f"{g}mk_{h}_{bi}" if z == len(ents) - 1 else None))
                w = [f"{g}sm_{h}_{last_si}"]
                if nmk:
                    w.append(f"{g}mk_{h}_{bi}")
                if qb > 0:
                    w.append(f"{g}avg_{h}_{2*qb-1}")
                elif h > 0:
                    w.append(f"{g}avg_{h-1}_31")
                rows = 64 if ents[0][3] else 128
                slot0 = ents[0][4]
                ph.emit("scalar", lambda s, r=r, nb=nb, rows=rows, slot0=slot0: s.activation(
                    abuf[0:rows, slot0 * 256:(slot0 + nb) * 256],
                    reg(r)[0:rows, 0:nb * 256], AF.Exp),
                    waits=w, ev=f"{g}ex_{h}_{bi}")
                if bi + 1 < len(batches) and batches[bi + 1][0] == qb:
                    continue
                for qh in range(2):
                    a = 2 * qb + qh
                    tiles = AVT[qb][qh]
                    w = sorted({f"{g}ex_{h}_{tile_batch[sidx[(qb, kt, ii)]]}" for (kt, ii) in tiles})
                    if any(ii for (kt, ii) in tiles):
                        w.append(f"{g}vil_{h}")
                    if a >= 2:
                        w.append(f"{g}otm_{h}_{a-2}")
                    elif h > 0:
                        w.append(f"{g}otm_{h-1}_{30+a}")
                    for z, (kt, ii) in enumerate(tiles):
                        slot = slot_of(qb, kt, ii)
                        lhs_rows = 63 if ii else 128
                        ph.emit("tensor", lambda t, qh=qh, kt=kt, ii=ii, slot=slot,
                                z=z, a=a, lhs_rows=lhs_rows, nt=len(tiles): t.matmul(
                            fb(4 + a % 2)[:, 0:65],
                            abuf[0:lhs_rows, slot * 256 + qh * 128: slot * 256 + qh * 128 + 128],
                            (vib[0:63, :] if ii else vtm[:, kt * 65:(kt + 1) * 65]),
                            start=(z == 0), stop=(z == nt - 1)),
                            waits=(w if z == 0 else []),
                            ev=(f"{g}avg_{h}_{a}" if z == len(tiles) - 1 else None))
                    w = [f"{g}avg_{h}_{a}"]
                    if a >= 2:
                        w.append(f"{g}otm_{h}_{a-2}")
                    elif h > 0:
                        w.append(f"{g}otm_{h-1}_{30+a}")
                    ph.emit("vector", lambda v, a=a: v.reciprocal(rcp[a % 2][:], fb(4 + a % 2)[:, 64:65]),
                            waits=w, ev=f"{g}rc_{h}_{a}")
                    w = [f"{g}rc_{h}_{a}"]
                    if a >= 2:
                        w.append(f"{g}otr_{h}_{a-2}")
                    elif h > 0:
                        w.append(f"{g}otr_{h-1}_{30+a}")
                    ph.emit("scalar", lambda v, a=a: v.activation(
                        otm[a % 2][:, 0:64], fb(4 + a % 2)[:, 0:64], AF.Identity,
                        scale=rcp[a % 2][:]),
                        waits=w, ev=f"{g}otm_{h}_{a}")
                    def emit_otr(a, h=h, hh=hh, p=p):
                        qb2, qh2 = a // 2, a % 2
                        w = [f"{g}otm_{h}_{a}", f"{g}vau_{h}_31"]
                        if a >= 2:
                            w.append(f"{g}oev_{h}_{a-2}")
                        ph.emit("tensor", lambda t, a=a: t.transpose(
                            bb(a % 2)[0:64, 128:256], otm[a % 2][:, 0:64], c.ident[:]),
                            waits=w, ev=f"{g}otr_{h}_{a}")
                        w = [f"{g}otr_{h}_{a}"]
                        if h >= 2 and a == 0:
                            w.append(f"{g}womm_{p-1}_63")
                        ph.emit("vector", lambda v, a=a, qb2=qb2, qh2=qh2, hh=hh: v.tensor_copy(
                            o_h[hh][0:64, qb2 * 256 + qh2 * 128: qb2 * 256 + qh2 * 128 + 128],
                            bb(a % 2)[0:64, 128:256]),
                            waits=w, ev=f"{g}oev_{h}_{a}")
                    if a > 0:
                        emit_otr(a - 1)
                    if a == 31:
                        emit_otr(31)

            if hh == 1:
                for j in range(64):
                    dc, tb = j // 8, j % 8
                    w = [f"{g}wol_{2*p}", f"{g}wol_{2*p+1}",
                         f"{g}oev_{2*p}_31", f"{g}oev_{2*p+1}_31"]
                    if j >= 4:
                        jj = j - 4
                        w.append(f"{g}woev_{p}_{jj}" if jj % 2 == 0 else f"{g}wog_{p}_{jj}")
                    else:
                        w.append(f"{g}ex_{h}_{NB-1}")
                    ph.emit("tensor", lambda t, dc=dc, tb=tb, j=j: t.matmul(
                        fb(j % 4), wo_b[0][:, dc * 128:(dc + 1) * 128],
                        o_h[0][0:64, tb * 512:(tb + 1) * 512], start=True, stop=False),
                        waits=w, ev=None)
                    ph.emit("tensor", lambda t, dc=dc, tb=tb, j=j: t.matmul(
                        fb(j % 4), wo_b[1][:, dc * 128:(dc + 1) * 128],
                        o_h[1][0:64, tb * 512:(tb + 1) * 512], start=False, stop=True),
                        waits=[], ev=f"{g}womm_{p}_{j}")
                    if j % 2 == 0:
                        ph.emit("vector", lambda v, dc=dc, tb=tb, j=j: v.tensor_add(
                            c.xt[:, dc * T + tb * 512: dc * T + tb * 512 + 512],
                            c.xt[:, dc * T + tb * 512: dc * T + tb * 512 + 512],
                            fb(j % 4)),
                            waits=[f"{g}womm_{p}_{j}"], ev=f"{g}woev_{p}_{j}")
                    else:
                        wv = [f"{g}womm_{p}_{j}"]
                        if j >= 4:
                            wv.append(f"{g}wog_{p}_{j-4}")
                        elif p > 0:
                            wv.append(f"{g}wog_{p-1}_{60+j}")
                        ph.emit("scalar", lambda sc, j=j: sc.copy(
                            wtmp[(j // 2) % 2][:], fb(j % 4)),
                            waits=wv, ev=f"{g}wcp_{p}_{j}")
                        ph.emit("gpsimd", lambda e, dc=dc, tb=tb, j=j: e.tensor_add(
                            c.xt[:, dc * T + tb * 512: dc * T + tb * 512 + 512],
                            c.xt[:, dc * T + tb * 512: dc * T + tb * 512 + 512],
                            wtmp[(j // 2) % 2][:]),
                            waits=[f"{g}wcp_{p}_{j}"], ev=f"{g}wog_{p}_{j}", sem="gc")
        if getattr(c, "dbg_attn", False):
            lastp = NH // 2 - 1
            w = [f"{g}womm_{lastp}_63", f"{g}ex_{NH-1}_{NB-1}"]
            ph.dma("sync", lambda e: e.dma_start(out=c.dbg_qa[:], in_=qa[:]), waits=w, ev=f"{g}dbg0")
            ph.dma("sync", lambda e: e.dma_start(out=c.dbg_ka[:], in_=ka[:]), waits=w, ev=f"{g}dbg1")
            ph.dma("sync", lambda e: e.dma_start(
                out=c.dbg_ab[:, 0:nsl * 256], in_=abuf[:]), waits=w, ev=f"{g}dbg2")
            ph.dma("sync", lambda e: e.dma_start(
                out=c.dbg_vt[:, 0:32 * 65], in_=vtm[:]), waits=w, ev=f"{g}dbg3")
            ph.emit("sync", None, waits=[f"{g}dbg3"])
        ph.seal()
        ph.build()



def build_program(phases, debug_yT=False, debug_attn=False):
    nc = bass.Bass("TRN2", target_bir_lowering=False)
    c = C()
    c.nc = nc
    c.uid = 0
    needed = set(ALWAYS_INPUTS)
    for p in phases:
        needed.update(PHASE_INPUTS[p])

    c.xT = nc.dram_tensor("xT", [D, T], F32, kind="ExternalInput")
    c.outT = nc.dram_tensor("outT", [D, T], F32, kind="ExternalOutput")
    c.dram = {}
    for name, shape, dt in COMMON_SPECS:
        if name in needed:
            c.dram[name] = nc.dram_tensor(name, shape, dt, kind="ExternalInput")

    if debug_yT:
        c.yT = nc.dram_tensor("yT", [D, T], BF16, kind="ExternalOutput")
    else:
        c.yT = nc.dram_tensor("yT", [D, T], BF16)
    c.dbg_attn = debug_attn
    if debug_attn:
        c.dbg_qa = nc.dram_tensor("dbg_qa", [65, T], BF16, kind="ExternalOutput")
        c.dbg_ka = nc.dram_tensor("dbg_ka", [65, T], BF16, kind="ExternalOutput")
        c.dbg_ab = nc.dram_tensor("dbg_ab", [128, 32 * 256], BF16, kind="ExternalOutput")
        c.dbg_vt = nc.dram_tensor("dbg_vt", [128, 32 * 65], BF16, kind="ExternalOutput")
    if "pre_l0ffn" in phases:
        c.wg_bf = nc.dram_tensor("wg_bf", [NE, 32, 128, 1024], BF16)
        c.wu_bf = nc.dram_tensor("wu_bf", [NE, 32, 128, 1024], BF16)
        c.wd_bf = nc.dram_tensor("wd_bf", [NE, FF, D], BF16)
    if "pre_l0attn" in phases:
        c.l0_woT_bf = nc.dram_tensor("l0_woT_bf", [D, D], BF16)
    if "pre_l1attn" in phases:
        c.l1_wqT_bf = nc.dram_tensor("l1_wqT_bf", [D, D], BF16)
        c.l1_wkT_bf = nc.dram_tensor("l1_wkT_bf", [D, D], BF16)
        c.l1_wvT_bf = nc.dram_tensor("l1_wvT_bf", [D, D], BF16)
        c.l1_woT_bf = nc.dram_tensor("l1_woT_bf", [D, D], BF16)
        c.ki_bf = nc.dram_tensor("ki_bf", [NH, 64, 63], BF16)
        c.vi_bf = nc.dram_tensor("vi_bf", [NH, 63, 64], BF16)
    if "pre_l1ffn" in phases:
        c.w1s = nc.dram_tensor("w1s", [32, 128, 1024], BF16)
        c.w2s = nc.dram_tensor("w2s", [8, 128, 4096], BF16)
        c.alpha1d = nc.dram_tensor("alpha1d", [32, 128, 1], F32)
        c.alpha2d = nc.dram_tensor("alpha2d", [8, 128, 1], F32)

    with ExitStack() as ctx:
        c.xt = ctx.enter_context(nc.sbuf_tensor(f"xt_{c.uid}", [128, 8 * T], F32))
        c.ones = ctx.enter_context(nc.sbuf_tensor(f"c_ones_{c.uid}", [128, 128], BF16))
        c.ident = ctx.enter_context(nc.sbuf_tensor(f"c_ident_{c.uid}", [128, 128], BF16))
        c.eps = ctx.enter_context(nc.sbuf_tensor(f"c_eps_{c.uid}", [128, 1], F32))
        c.tiny = ctx.enter_context(nc.sbuf_tensor(f"c_tiny_{c.uid}", [128, 1], F32))
        if "pre_l0attn" in phases:
            c.l0wq = ctx.enter_context(nc.sbuf_tensor(f"l0wq_{c.uid}", [64, 64], BF16))
            c.l0wk = ctx.enter_context(nc.sbuf_tensor(f"l0wk_{c.uid}", [64, 64], BF16))
            c.l0wv = ctx.enter_context(nc.sbuf_tensor(f"l0wv_{c.uid}", [64, 64], BF16))
        if "l0attn" in phases:
            c.l0m = ctx.enter_context(nc.sbuf_tensor(f"l0m_{c.uid}", [128, 512], F32))
        if "l1attn" in phases:
            c.l1m = ctx.enter_context(nc.sbuf_tensor(f"l1m_{c.uid}", [128, 768], F32))
            c.l1mi = ctx.enter_context(nc.sbuf_tensor(f"l1mi_{c.uid}", [64, 256], F32))
        if "pre_l1ffn" in phases:
            c.a1 = ctx.enter_context(nc.sbuf_tensor(f"a1_{c.uid}", [128, 32], F32))
            c.b1 = ctx.enter_context(nc.sbuf_tensor(f"b1_{c.uid}", [128, 32], F32))
            c.a2h = ctx.enter_context(nc.sbuf_tensor(f"a2h_{c.uid}", [128, 8], F32))
            c.b2h = ctx.enter_context(nc.sbuf_tensor(f"b2h_{c.uid}", [128, 8], F32))

        c.semh = {}
        c.semn = {s: 0 for s in SEMS}
        c.sems = {}
        for s in SEMS:
            c.semh[s] = ctx.enter_context(nc.semaphore("sem_" + s))

        ph_init(c, needed)
        counters = {}
        for p in phases:
            counters[p] = counters.get(p, 0) + 1
            tag = f"{p}{counters[p]}_"
            if p == "ln":
                ph_ln(c, tag)
            elif p == "pre_l0ffn":
                ph_pre_l0ffn(c)
            elif p == "pre_l0attn":
                ph_pre_small(c, "l0attn")
            elif p == "pre_l1attn":
                ph_pre_small(c, "l1attn")
            elif p == "pre_l1ffn":
                ph_pre_l1ffn(c)
            elif p == "l0ffn":
                ph_l0ffn(c, tag)
            elif p == "l0attn":
                ph_attn(c, 0, tag)
            elif p == "l1attn":
                ph_attn(c, 1, tag)
            elif p == "l1ffn":
                ph_l1ffn(c, tag)
            else:
                raise ValueError(p)
        ph_store(c)

    input_names = ["xT"] + sorted(c.dram.keys())
    return nc, input_names



def _run(nc, input_names, common, xTs):
    in_maps = []
    for cid in range(NCORE):
        m = {}
        for n in input_names:
            m[n] = xTs[cid] if n == "xT" else common[n]
        in_maps.append(m)
    res = run_bass_kernel_spmd(nc, in_maps, core_ids=list(range(NCORE)))
    return [np.asarray(r["outT"]) for r in res.results]


LAUNCHES = [
    ["pre_l0ffn", "ln", "l0ffn"],
    ["pre_l0attn", "ln", "l0attn"],
    ["pre_l0ffn", "ln", "l0ffn"],
    ["pre_l1ffn", "ln", "l1ffn"],
    ["pre_l1attn", "ln", "l1attn"],
    ["pre_l1ffn", "ln", "l1ffn"],
]

FUSED_PHASES = [
    "pre_l0ffn", "pre_l0attn", "pre_l1attn", "pre_l1ffn",
    "ln", "l0ffn",
    "ln", "l0attn",
    "ln", "l0ffn",
    "ln", "l1ffn",
    "ln", "l1attn",
    "ln", "l1ffn",
]

FUSED = True


def kernel(**inputs):
    common = prep_common(inputs)
    x = np.asarray(inputs["L0_attn_x"], dtype=np.float32)
    xTs = [np.ascontiguousarray(x[b].T) for b in range(NCORE)]
    if FUSED:
        nc, names = build_program(list(FUSED_PHASES))
        cur = _run(nc, names, common, xTs)
    else:
        progs = {}
        for ph in LAUNCHES:
            key = tuple(ph)
            if key not in progs:
                progs[key] = build_program(list(ph))
        cur = xTs
        for ph in LAUNCHES:
            nc, names = progs[tuple(ph)]
            cur = _run(nc, names, common, cur)
    out = np.stack([np.ascontiguousarray(np.asarray(t, np.float32).T) for t in cur])
    return np.ascontiguousarray(out)
